# version 03
# baseline (speedup 1.0000x reference)
import numpy as np
from contextlib import ExitStack

import concourse.bass as bass
import concourse.mybir as mybir
from concourse.bass_utils import run_bass_kernel_spmd

F32 = mybir.dt.float32
BF16 = mybir.dt.bfloat16
AF = mybir.ActivationFunctionType
OP = mybir.AluOpType
AX = mybir.AxisListType

D = 1024
KT = 8
EPS = 1e-5

_WRAP = {
    "tensor_copy", "tensor_tensor", "tensor_add", "tensor_sub", "tensor_mul",
    "tensor_max", "tensor_scalar", "tensor_single_scalar", "tensor_scalar_add",
    "tensor_scalar_sub", "tensor_scalar_mul", "tensor_scalar_max",
    "tensor_scalar_min", "tensor_relu", "tensor_reduce", "reduce_max",
    "reduce_sum", "scalar_tensor_tensor", "tensor_tensor_scan", "memset",
    "memzero", "activation", "sign", "square", "sqrt", "mul", "copy", "add",
    "max", "max_index", "match_replace", "select", "copy_predicated",
    "reciprocal", "bn_stats", "bn_aggr", "transpose", "pool", "pool_max",
    "pool_avg", "stream_shuffle", "iota", "affine_select",
}


class Sems:
    def __init__(self, nc, ctx, names):
        self.h = {n: ctx.enter_context(nc.semaphore(n)) for n in names}
        self.c = {n: 0 for n in names}

    def inc(self, inst, name, k=1):
        self.c[name] += k
        inst.then_inc(self.h[name], k)
        return self.c[name]

    def wait(self, eng, name, val=None):
        v = self.c[name] if val is None else val
        if v > 0:
            e = eng.e if isinstance(eng, Eng) else eng
            e.wait_ge(self.h[name], v)

    def mark(self, name):
        return self.c[name]


class Eng:

    def __init__(self, eng, s, ch):
        self.e = eng
        self.s = s
        self.ch = ch

    def __getattr__(self, k):
        f = getattr(self.e, k)
        if k in _WRAP:
            s, ch, e = self.s, self.ch, self.e

            def g(*a, **kw):
                n = s.c[ch]
                if n:
                    e.wait_ge(s.h[ch], n)
                inst = f(*a, **kw)
                s.c[ch] += 1
                inst.then_inc(s.h[ch], 1)
                return inst
            return g
        return f


class P:
    def __init__(self, name):
        self.nc = bass.Bass("TRN2", target_bir_lowering=False)
        self.ctx = ExitStack()
        self.name = name

    def sb(self, name, shape, dtype):
        return self.ctx.enter_context(self.nc.sbuf_tensor(name, shape, dtype))

    def ps(self, name, shape, dtype=F32):
        return self.ctx.enter_context(self.nc.psum_tensor(name, shape, dtype))

    def engines(self, sem_names):
        nc = self.nc
        s = Sems(nc, self.ctx, ["v", "a", "g", "t"] + sem_names)
        self.s = s
        self.blk = self.ctx.enter_context(nc.Block())
        self.sync = nc.sync
        self.gpsd = nc.gpsimd
        self.ten = nc.tensor
        self.vec = Eng(nc.vector, s, "v")
        self.act = Eng(nc.scalar, s, "a")
        self.gpc = Eng(nc.gpsimd, s, "g")
        return s


def emit_ln(p, s, T, x_hbm, xn, xch, xbf, xsq, stats, t1, ones_bf, ones_f,
            pb_s, pb_ss, pb_mb, pb_rb, xd="xd"):
    sync, ten, vec, act = p.sync, p.ten, p.vec, p.act
    for c in range(T // 512):
        sl = slice(c * 512, (c + 1) * 512)
        s.wait(sync, "v")
        last = sync.dma_start(out=xch[:, :, :],
                              in_=x_hbm[:, sl].rearrange("(k p) t -> p k t", p=128))
        s.inc(last, xd, 16)
        s.wait(vec, xd)
        s.wait(vec, "t")
        for kt in range(KT):
            vec.tensor_copy(xbf[:, :], xch[:, kt, :])
            vec.tensor_mul(xsq[:, :], xbf[:, :], xbf[:, :])
            s.wait(ten, "v")
            ten.matmul(pb_s[0:1, :], ones_bf[:, :], xbf[:, :],
                       start=(kt == 0), stop=(kt == KT - 1))
            last = ten.matmul(pb_ss[0:1, :], ones_bf[:, :], xsq[:, :],
                              start=(kt == 0), stop=(kt == KT - 1))
            s.inc(last, "t")
            s.wait(vec, "t")
        vec.tensor_scalar(stats[0:1, 2, :], pb_s[0:1, :], 1.0 / D, None, op0=OP.mult)
        vec.tensor_scalar(stats[0:1, 1, :], pb_ss[0:1, :], 1.0 / D, None, op0=OP.mult)
        vec.tensor_mul(stats[0:1, 3, :], stats[0:1, 2, :], stats[0:1, 2, :])
        vec.tensor_sub(stats[0:1, 3, :], stats[0:1, 1, :], stats[0:1, 3, :])
        vec.tensor_scalar(stats[0:1, 3, :], stats[0:1, 3, :], EPS, None, op0=OP.add)
        s.wait(act, "v")
        act.activation(stats[0:1, 4, :], stats[0:1, 3, :], AF.Ln)
        act.activation(stats[0:1, 4, :], stats[0:1, 4, :], AF.Exp, scale=-0.5)
        s.wait(ten, "a")
        s.wait(ten, "v")
        ten.matmul(pb_mb[:, :], ones_f[:, :], stats[0:1, 2, :], start=True, stop=True)
        last = ten.matmul(pb_rb[:, :], ones_f[:, :], stats[0:1, 4, :], start=True, stop=True)
        s.inc(last, "t")
        s.wait(vec, "t")
        for kt in range(KT):
            vec.tensor_sub(t1[:, :], xch[:, kt, :], pb_mb[:, :])
            vec.tensor_mul(xn[:, kt, sl], t1[:, :], pb_rb[:, :])


def build_moe(T=2048, TH=1024, E=9, I=2048):
    IT = I // 128
    NH = T // TH
    p = P("moe")
    nc = p.nc
    with p.ctx:
        x = nc.dram_tensor("x", [D, T], F32, kind="ExternalInput").ap()
        wg = nc.dram_tensor("wg", [E, IT, KT, 128, 128], F32, kind="ExternalInput").ap()
        wu = nc.dram_tensor("wu", [E, IT, KT, 128, 128], F32, kind="ExternalInput").ap()
        wdr = nc.dram_tensor("wdr", [E, KT, IT, 128, 128], F32, kind="ExternalInput").ap()
        wr = nc.dram_tensor("wr", [D, 8], F32, kind="ExternalInput").ap()
        rbv = nc.dram_tensor("rbv", [1, 8], F32, kind="ExternalInput").ap()
        identf = nc.dram_tensor("identf", [128, 128], F32, kind="ExternalInput").ap()
        y = nc.dram_tensor("y", [D, T], F32, kind="ExternalOutput").ap()

        xn = p.sb("xn", [128, KT, T], BF16)
        xch = p.sb("xch", [128, KT, 512], F32)
        xr = p.sb("xr", [128, 512], F32)
        xbf = p.sb("xbf", [128, 512], BF16)
        xsq = p.sb("xsq", [128, 512], BF16)
        stats = p.sb("stats", [1, 5, 512], F32)
        t1 = p.sb("t1", [128, 512], F32)
        ones_bf = p.sb("ones_bf", [128, 1], BF16)
        ones_f = p.sb("ones_f", [1, 128], F32)
        ones_bf1 = p.sb("ones_bf1", [1, 128], BF16)
        ident = p.sb("ident", [128, 128], F32)
        rbt = p.sb("rbt", [128, 8], F32)
        etmp = p.sb("etmp", [128, 8], F32)
        probs = p.sb("probs", [128, 8], F32)
        m8 = p.sb("m8", [128, 8], F32)
        cwt = p.sb("cwt", [128, 8], F32)
        cwT = p.sb("cwT", [1, 8, T], BF16)
        cwbc = p.sb("cwbc", [128, 512], BF16)
        wrsb = p.sb("wrsb", [128, KT, 8], BF16)
        xe = p.sb("xe", [128, KT, TH], BF16)
        wgsb = [p.sb(f"wgsb{i}", [128, KT, 128], BF16) for i in range(2)]
        wusb = [p.sb(f"wusb{i}", [128, KT, 128], BF16) for i in range(2)]
        wdsb = [p.sb(f"wdsb{i}", [128, IT, 128], BF16) for i in range(2)]
        sg = p.sb("sg", [128, 512], BF16)
        gu = p.sb("gu", [128, IT, TH], BF16)
        yac = p.sb("yac", [128, KT, TH], F32)

        pb = [p.ps(f"pb{i}", [128, 512]) for i in range(6)]

        s = p.engines(["xd", "wl0", "wl1", "wd0", "wd1", "wr0", "st"])
        sync, gps, ten = p.sync, p.gpsd, p.ten
        vec, act, gpc = p.vec, p.act, p.gpc

        vec.memset(ones_bf[:, :], 1.0)
        vec.memset(ones_f[:, :], 1.0)
        vec.memset(ones_bf1[:, :], 1.0)
        last = sync.dma_start(out=ident[:, :], in_=identf)
        s.inc(last, "xd", 16)
        last = sync.dma_start(out=rbt[:, :], in_=rbv.partition_broadcast(128))
        s.inc(last, "xd", 16)
        last = gps.dma_start(out=wrsb[:, :, :],
                             in_=wr.rearrange("(k p) e -> p k e", p=128))
        s.inc(last, "wr0", 16)

        emit_ln(p, s, T, x, xn, xch, xbf, xsq, stats, t1, ones_bf, ones_f,
                pb[0][0:1, :], pb[3][0:1, :], pb[1], pb[2])

        s.wait(ten, "wr0")
        s.wait(ten, "xd")
        s.wait(ten, "v")
        for tt in range(T // 128):
            tsl = slice(tt * 128, (tt + 1) * 128)
            for kt in range(KT):
                lastm = ten.matmul(pb[3][:, 0:8], xn[:, kt, tsl], wrsb[:, kt, :],
                                   start=(kt == 0), stop=(kt == KT - 1))
            s.inc(lastm, "t")
            s.wait(vec, "t")
            vec.tensor_add(etmp[:, :], pb[3][:, 0:8], rbt[:, :])
            s.wait(act, "v")
            act.activation(etmp[:, :], etmp[:, :], AF.Exp, scale=-1.0)
            s.wait(vec, "a")
            vec.tensor_scalar(etmp[:, :], etmp[:, :], 1.0, None, op0=OP.add)
            vec.reciprocal(probs[:, :], etmp[:, :])
            vec.max(out=m8[:, :], in_=probs[:, :])
            vec.tensor_scalar(cwt[:, :], probs[:, :], m8[:, 1:2], None, op0=OP.is_ge)
            vec.tensor_mul(cwt[:, :], probs[:, :], cwt[:, :])
            s.wait(ten, "v")
            for half8 in range(2):
                for e8 in range(4):
                    ee = half8 * 4 + e8
                    last = ten.transpose(pb[4][0:1, e8 * 128:(e8 + 1) * 128],
                                         cwt[:, ee:ee + 1], ident[:, :])
                s.inc(last, "t")
                s.wait(vec, "t")
                for e8 in range(4):
                    ee = half8 * 4 + e8
                    vec.tensor_copy(cwT[0:1, ee, tsl],
                                    pb[4][0:1, e8 * 128:(e8 + 1) * 128])
                s.wait(ten, "v")

        wslot_mm = [0, 0]
        wd_slot_mm = [0, 0]
        xe_last_mm = 0
        for h in range(NH):
            s.wait(vec, "t")
            vec.memset(yac[:, :, :], 0.0)
            for e in range(E):
                if e > 0:
                    s.wait(vec, "t", xe_last_mm)
                    for c in range(TH // 512):
                        csl = slice(h * TH + c * 512, h * TH + (c + 1) * 512)
                        osl = slice(c * 512, (c + 1) * 512)
                        s.wait(ten, "v")
                        lastm = ten.matmul(pb[5][:, :], ones_bf1[:, :],
                                           cwT[0:1, e - 1, csl], start=True, stop=True)
                        s.inc(lastm, "t")
                        s.wait(vec, "t")
                        vec.tensor_copy(cwbc[:, :], pb[5][:, :])
                        for kt in range(KT):
                            vec.tensor_mul(xe[:, kt, osl], xn[:, kt, csl], cwbc[:, :])
                    s.wait(ten, "v")
                for ii in range(IT):
                    sl_i = ii % 2
                    ch = f"wl{sl_i}"
                    s.wait(gps, "t", wslot_mm[sl_i])
                    lw = gps.dma_start(out=wgsb[sl_i][:, :, :],
                                       in_=wg[e, ii].rearrange("k p c -> p k c"))
                    s.inc(lw, ch, 16)
                    lw = gps.dma_start(out=wusb[sl_i][:, :, :],
                                       in_=wu[e, ii].rearrange("k p c -> p k c"))
                    s.inc(lw, ch, 16)
                    s.wait(ten, ch)
                    for c in range(TH // 512):
                        osl = slice(c * 512, (c + 1) * 512)
                        csl = slice(h * TH + c * 512, h * TH + (c + 1) * 512)
                        s.wait(ten, "v")
                        s.wait(ten, "a")
                        for kt in range(KT):
                            ten.matmul(pb[0][:, :], wgsb[sl_i][:, kt, :],
                                       xn[:, kt, csl],
                                       start=(kt == 0), stop=(kt == KT - 1))
                        for kt in range(KT):
                            rv = xn[:, kt, csl] if e == 0 else xe[:, kt, osl]
                            lastm = ten.matmul(pb[2][:, :], wusb[sl_i][:, kt, :], rv,
                                               start=(kt == 0), stop=(kt == KT - 1))
                        s.inc(lastm, "t")
                        s.wait(act, "t")
                        s.wait(act, "v")
                        act.activation(sg[:, :], pb[0][:, :], AF.Silu)
                        s.wait(vec, "a")
                        vec.tensor_mul(gu[:, ii, osl], sg[:, :], pb[2][:, :])
                    wslot_mm[sl_i] = s.mark("t")
                xe_last_mm = s.mark("t")
                for oo in range(KT):
                    sl_o = oo % 2
                    ch = f"wd{sl_o}"
                    s.wait(gps, "t", wd_slot_mm[sl_o])
                    lw = gps.dma_start(out=wdsb[sl_o][:, :, :],
                                       in_=wdr[e, oo].rearrange("i p c -> p i c"))
                    s.inc(lw, ch, 16)
                    s.wait(ten, ch)
                    if oo == 0:
                        s.wait(ten, "v")
                    for c in range(TH // 512):
                        osl = slice(c * 512, (c + 1) * 512)
                        for ii in range(IT):
                            lastm = ten.matmul(
                                pb[4][:, :], wdsb[sl_o][:, ii, :], gu[:, ii, osl],
                                start=(ii == 0), stop=(ii == IT - 1))
                        s.inc(lastm, "t")
                        s.wait(vec, "t")
                        vec.tensor_add(yac[:, oo, osl], yac[:, oo, osl], pb[4][:, :])
                        s.wait(ten, "v")
                    wd_slot_mm[sl_o] = s.mark("t")
            for kt in range(KT):
                for c in range(TH // 512):
                    osl = slice(c * 512, (c + 1) * 512)
                    csl = slice(h * TH + c * 512, h * TH + (c + 1) * 512)
                    s.wait(sync, "v")
                    last = sync.dma_start(out=xr[:, :],
                                          in_=x[kt * 128:(kt + 1) * 128, csl])
                    s.inc(last, "xd", 16)
                    s.wait(vec, "xd")
                    s.wait(vec, "st")
                    vec.tensor_add(t1[:, :], xr[:, :], yac[:, kt, osl])
                    s.wait(sync, "v")
                    last = sync.dma_start(out=y[kt * 128:(kt + 1) * 128, csl],
                                          in_=t1[:, :])
                    s.inc(last, "st", 16)
        s.wait(sync, "st")
        s.wait(gps, "st")
    return nc



def tile_wT(wT, IT_out=None):
    ind, outd = wT.shape
    t = wT.reshape(ind // 128, 128, outd // 128, 128)
    return np.ascontiguousarray(t.transpose(2, 0, 1, 3))


def host_moe_inputs(inp, prefix="L0_ffn"):
    Wg_s = inp[f"{prefix}_Wg_s"]; Wu_s = inp[f"{prefix}_Wu_s"]; Wd_s = inp[f"{prefix}_Wd_s"]
    Wg = inp[f"{prefix}_Wg"]; Wu = inp[f"{prefix}_Wu"]; Wd = inp[f"{prefix}_Wd"]
    Wr = inp[f"{prefix}_Wr"]; rb = inp[f"{prefix}_rb"]
    wg = np.stack([tile_wT(np.ascontiguousarray(w.T))
                   for w in [Wg_s] + list(Wg)]).astype(np.float32)
    wu = np.stack([tile_wT(np.ascontiguousarray(w.T))
                   for w in [Wu_s] + list(Wu)]).astype(np.float32)
    wdr = np.stack([tile_wT(np.ascontiguousarray(w.T))
                    for w in [Wd_s] + list(Wd)]).astype(np.float32)
    return {
        "wg": wg, "wu": wu, "wdr": wdr,
        "wr": np.ascontiguousarray(Wr.T).astype(np.float32),
        "rbv": np.asarray(rb).reshape(1, 8).astype(np.float32),
        "identf": np.eye(128, dtype=np.float32),
    }


def build_attn_a(S=4096, Q=2048):
    H = 16
    NKT2 = S // 128
    NQC = Q // 512
    p = P("attn_a")
    nc = p.nc
    with p.ctx:
        xf = nc.dram_tensor("xf", [D, S], F32, kind="ExternalInput").ap()
        wqkv = nc.dram_tensor("wqkv", [D, 3072], F32, kind="ExternalInput").ap()
        wproj = nc.dram_tensor("wproj", [D, D], F32, kind="ExternalInput").ap()
        bproj = nc.dram_tensor("bproj", [D, 1], F32, kind="ExternalInput").ap()
        exb = nc.dram_tensor("exb", [S, Q], BF16, kind="ExternalInput").ap()
        y = nc.dram_tensor("y", [D, Q], F32, kind="ExternalOutput").ap()

        xn = p.sb("xn", [128, KT, S], BF16)
        xch = p.sb("xch", [128, KT, 512], F32)
        xr = p.sb("xr", [128, 512], F32)
        xbf = p.sb("xbf", [128, 512], BF16)
        xsq = p.sb("xsq", [128, 512], BF16)
        stats = p.sb("stats", [1, 5, 512], F32)
        t1 = p.sb("t1", [128, 512], F32)
        ones_bf = p.sb("ones_bf", [128, 1], BF16)
        ones_f = p.sb("ones_f", [1, 128], F32)
        wsb = [p.sb(f"wsb{i}", [128, KT, 128], BF16) for i in range(2)]
        kfm = p.sb("kfm", [128, S], BF16)
        qfm = p.sb("qfm", [128, Q], BF16)
        vtm = p.sb("vtm", [128, NKT2, 130], BF16)
        exbs = p.sb("exbs", [128, NKT2, 512], BF16)
        ptile = [p.sb(f"ptile{i}", [128, 512], BF16) for i in range(2)]
        ofm = p.sb("ofm", [128, KT, Q], BF16)
        osb = p.sb("osb", [128, 512], BF16)
        rd1 = p.sb("rd1", [1, 512], F32)

        pb = [p.ps(f"pb{i}", [128, 512]) for i in range(6)]

        s = p.engines(["xd", "wl0", "wl1", "eb", "st"])
        sync, gps, ten = p.sync, p.gpsd, p.ten
        vec, act = p.vec, p.act

        vec.memset(ones_bf[:, :], 1.0)
        vec.memset(ones_f[:, :], 1.0)
        vec.memset(vtm[:, :, 64:65], 1.0)
        vec.memset(vtm[:, :, 129:130], 1.0)

        emit_ln(p, s, S, xf, xn, xch, xbf, xsq, stats, t1, ones_bf, ones_f,
                pb[0][0:1, :], pb[3][0:1, :], pb[1], pb[2])

        wslot_mm = [0, 0]

        def load_w(cols_off, slot, ch):
            s.wait(gps, "t", wslot_mm[slot])
            lw = gps.dma_start(
                out=wsb[slot][:, :, :],
                in_=wqkv[:, cols_off:cols_off + 128].rearrange(
                    "(k p) c -> p k c", p=128))
            s.inc(lw, ch, 16)

        s.wait(ten, "v")
        for hp in range(8):
            load_w(1024 + hp * 128, 0, "wl0")
            s.wait(ten, "wl0")
            s.wait(ten, "a")
            s.wait(ten, "v")
            for c in range(S // 512):
                sl = slice(c * 512, (c + 1) * 512)
                s.wait(ten, "v")
                for kt in range(KT):
                    lastm = ten.matmul(pb[4][:, :], wsb[0][:, kt, :], xn[:, kt, sl],
                                       start=(kt == 0), stop=(kt == KT - 1))
                s.inc(lastm, "t")
                s.wait(vec, "t")
                vec.tensor_copy(kfm[:, sl], pb[4][:, :])
            wslot_mm[0] = s.mark("t")
            load_w(hp * 128, 1, "wl1")
            s.wait(ten, "wl1")
            s.wait(ten, "v")
            for c in range(NQC):
                sl = slice(c * 512, (c + 1) * 512)
                s.wait(ten, "v")
                for kt in range(KT):
                    lastm = ten.matmul(pb[5][:, :], wsb[1][:, kt, :], xn[:, kt, sl],
                                       start=(kt == 0), stop=(kt == KT - 1))
                s.inc(lastm, "t")
                s.wait(vec, "t")
                vec.tensor_copy(qfm[:, sl], pb[5][:, :])
            wslot_mm[1] = s.mark("t")
            load_w(2048 + hp * 128, 0, "wl0")
            s.wait(ten, "wl0")
            s.wait(ten, "v")
            for tt in range(NKT2):
                tsl = slice(tt * 128, (tt + 1) * 128)
                s.wait(ten, "v")
                for kt in range(KT):
                    lastm = ten.matmul(pb[4][:, 0:128], xn[:, kt, tsl],
                                       wsb[0][:, kt, :],
                                       start=(kt == 0), stop=(kt == KT - 1))
                s.inc(lastm, "t")
                s.wait(vec, "t")
                vec.tensor_copy(vtm[:, tt, 0:64], pb[4][:, 0:64])
                vec.tensor_copy(vtm[:, tt, 65:129], pb[4][:, 64:128])
            wslot_mm[0] = s.mark("t")
            s.wait(ten, "v")

            for qc in range(NQC):
                qsl = slice(qc * 512, (qc + 1) * 512)
                s.wait(sync, "v")
                last = sync.dma_start(
                    out=exbs[:, :, :],
                    in_=exb[:, qsl].rearrange("(kt p) q -> p kt q", p=128))
                s.inc(last, "eb", 16)
                s.wait(vec, "eb")
                for h2 in range(2):
                    hb = 64 * h2
                    opb = pb[2]
                    pv_done_v = [0, 0]
                    for kt2 in range(NKT2 + 1):
                        if kt2 < NKT2:
                            sb_ = kt2 % 2
                            s.wait(ten, "a", s.mark("a") if kt2 >= 2 else 0)
                            lastm = ten.matmul(
                                pb[sb_][:, :],
                                kfm[hb:hb + 64, kt2 * 128:(kt2 + 1) * 128],
                                qfm[hb:hb + 64, qsl],
                                start=True, stop=True, skip_group_check=True)
                            s.inc(lastm, "t")
                            s.wait(act, "t")
                            s.wait(act, "v", pv_done_v[sb_])
                            act.activation(ptile[sb_][:, :], pb[sb_][:, :],
                                           AF.Exp, scale=0.125)
                            s.wait(vec, "a")
                            vec.tensor_mul(ptile[sb_][:, :], ptile[sb_][:, :],
                                           exbs[:, kt2, :])
                            pv_done_v[sb_] = s.mark("v")
                        if kt2 > 0:
                            kk = kt2 - 1
                            sb_ = kk % 2
                            s.wait(ten, "v", pv_done_v[sb_])
                            lastm = ten.matmul(
                                opb[0:65, :],
                                vtm[:, kk, 65 * h2:65 * h2 + 65],
                                ptile[sb_][:, :],
                                start=(kk == 0), stop=(kk == NKT2 - 1),
                                skip_group_check=True)
                            s.inc(lastm, "t")
                    s.wait(vec, "t")
                    vec.reciprocal(rd1[0:1, :], opb[64:65, :])
                    s.wait(ten, "v")
                    lastm = ten.matmul(pb[3][0:64, :], ones_f[:, 0:64],
                                       rd1[0:1, :], start=True, stop=True)
                    s.inc(lastm, "t")
                    s.wait(vec, "t")
                    vec.tensor_copy(osb[0:64, :], opb[0:64, :])
                    h_global = hp * 2 + h2
                    otile = h_global // 2
                    orow = 64 * (h_global % 2)
                    vec.tensor_mul(ofm[orow:orow + 64, otile, qsl],
                                   osb[0:64, :], pb[3][0:64, :])
                    s.wait(ten, "v")

        bsb = p.sb("bsb", [128, KT, 1], F32)
        last = sync.dma_start(out=bsb[:, :, :],
                              in_=bproj.rearrange("(k p) o -> p k o", p=128),
                              allow_slow_non_contiguous=True)
        s.inc(last, "xd", 16)
        wslot2 = list(wslot_mm)
        for ot in range(KT):
            slot = ot % 2
            ch = f"wl{slot}"
            s.wait(gps, "t", wslot2[slot])
            lw = gps.dma_start(
                out=wsb[slot][:, :, :],
                in_=wproj[:, ot * 128:(ot + 1) * 128].rearrange(
                    "(k p) c -> p k c", p=128))
            s.inc(lw, ch, 16)
            s.wait(ten, ch)
            for c in range(NQC):
                sl = slice(c * 512, (c + 1) * 512)
                s.wait(ten, "v")
                for kt in range(KT):
                    lastm = ten.matmul(pb[4][:, :], wsb[slot][:, kt, :],
                                       ofm[:, kt, sl], start=(kt == 0),
                                       stop=(kt == KT - 1))
                s.inc(lastm, "t")
                s.wait(sync, "v")
                last = sync.dma_start(out=xr[:, :],
                                      in_=xf[ot * 128:(ot + 1) * 128, sl])
                s.inc(last, "xd", 16)
                s.wait(vec, "xd")
                s.wait(vec, "t")
                s.wait(vec, "st")
                vec.tensor_scalar(t1[:, :], pb[4][:, :], bsb[:, ot, 0:1], None,
                                  op0=OP.add)
                vec.tensor_add(t1[:, :], t1[:, :], xr[:, :])
                s.wait(sync, "v")
                last = sync.dma_start(out=y[ot * 128:(ot + 1) * 128, sl],
                                      in_=t1[:, :])
                s.inc(last, "st", 16)
            wslot2[slot] = s.mark("t")
        s.wait(sync, "st")
        s.wait(gps, "st")
    return nc


def host_attn_a_inputs(inp):
    PHI = (1.0 + np.sqrt(5.0)) / 2.0
    S = 4096
    pos = (np.arange(S, dtype=np.float64) * PHI) % 1.0
    dist = np.abs(pos[:, None] - pos[None, :])
    exb_full = (np.exp(-dist) + 1e-8).astype(np.float32)
    import ml_dtypes
    return {
        "wqkv": np.ascontiguousarray(inp["L0_attn_a_w_qkv"]).astype(np.float32),
        "wproj": np.ascontiguousarray(inp["L0_attn_a_w_proj"]).astype(np.float32),
        "bproj": np.asarray(inp["L0_attn_a_b_proj"]).reshape(D, 1).astype(np.float32),
    }, exb_full


def build_attn_a1(S=4096, Q=2048):
    NKT2 = S // 128
    NQC = Q // 512
    p = P("attn_a1")
    nc = p.nc
    with p.ctx:
        xf = nc.dram_tensor("xf", [D, S], F32, kind="ExternalInput").ap()
        xq = nc.dram_tensor("xq", [D, Q], F32, kind="ExternalInput").ap()
        wq = nc.dram_tensor("wq", [D, 64], F32, kind="ExternalInput").ap()
        wk = nc.dram_tensor("wk", [D, 64], F32, kind="ExternalInput").ap()
        wv = nc.dram_tensor("wv", [D, D], F32, kind="ExternalInput").ap()
        cm = nc.dram_tensor("cm", [S, Q], BF16, kind="ExternalInput").ap()
        y = nc.dram_tensor("y", [D, Q], F32, kind="ExternalOutput").ap()

        xn = p.sb("xn", [128, KT, S], BF16)
        xnq = p.sb("xnq", [128, KT, Q], BF16)
        xch = p.sb("xch", [128, KT, 512], F32)
        xr = p.sb("xr", [128, 512], F32)
        xbf = p.sb("xbf", [128, 512], BF16)
        xsq = p.sb("xsq", [128, 512], BF16)
        stats = p.sb("stats", [1, 5, 512], F32)
        t1 = p.sb("t1", [128, 512], F32)
        ones_bf = p.sb("ones_bf", [128, 1], BF16)
        ones_f = p.sb("ones_f", [1, 128], F32)
        wqk_sb = p.sb("wqk_sb", [128, KT, 128], BF16)
        wv_sb = p.sb("wv_sb", [128, KT, 256], BF16)
        kfm = p.sb("kfm", [64, S], BF16)
        qfm = p.sb("qfm", [64, Q], BF16)
        vtm = p.sb("vtm", [128, NKT2, 256], BF16)
        cms = p.sb("cms", [128, NKT2, 512], BF16)
        rdq = p.sb("rdq", [1, NQC, 512], F32)
        osb = p.sb("osb", [128, 512], BF16)

        pb = [p.ps(f"pb{i}", [128, 512]) for i in range(6)]

        s = p.engines(["xd", "wl0", "wl1", "eb", "st"])
        sync, gps, ten = p.sync, p.gpsd, p.ten
        vec, act = p.vec, p.act

        vec.memset(ones_bf[:, :], 1.0)
        vec.memset(ones_f[:, :], 1.0)

        emit_ln(p, s, S, xf, xn, xch, xbf, xsq, stats, t1, ones_bf, ones_f,
                pb[0][0:1, :], pb[3][0:1, :], pb[1], pb[2])
        emit_ln(p, s, Q, xq, xnq, xch, xbf, xsq, stats, t1, ones_bf, ones_f,
                pb[0][0:1, :], pb[3][0:1, :], pb[1], pb[2])

        lw = gps.dma_start(out=wqk_sb[:, :, 0:64],
                           in_=wq.rearrange("(k p) c -> p k c", p=128))
        s.inc(lw, "wl0", 16)
        lw = gps.dma_start(out=wqk_sb[:, :, 64:128],
                           in_=wk.rearrange("(k p) c -> p k c", p=128))
        s.inc(lw, "wl0", 16)
        s.wait(ten, "wl0")
        s.wait(ten, "v")
        for c in range(NQC):
            sl = slice(c * 512, (c + 1) * 512)
            s.wait(ten, "v")
            for kt in range(KT):
                lastm = ten.matmul(pb[4][0:64, :], wqk_sb[:, kt, 0:64],
                                   xnq[:, kt, sl], start=(kt == 0),
                                   stop=(kt == KT - 1))
            s.inc(lastm, "t")
            s.wait(vec, "t")
            vec.tensor_copy(qfm[:, sl], pb[4][0:64, :])
        for c in range(S // 512):
            sl = slice(c * 512, (c + 1) * 512)
            s.wait(ten, "v")
            for kt in range(KT):
                lastm = ten.matmul(pb[4][0:64, :], wqk_sb[:, kt, 64:128],
                                   xn[:, kt, sl], start=(kt == 0),
                                   stop=(kt == KT - 1))
            s.inc(lastm, "t")
            s.wait(vec, "t")
            vec.tensor_copy(kfm[:, sl], pb[4][0:64, :])

        vload_mark = 0
        for vh in range(4):
            s.wait(gps, "t", vload_mark)
            lw = gps.dma_start(
                out=wv_sb[:, :, :],
                in_=wv[:, vh * 256:(vh + 1) * 256].rearrange(
                    "(k p) c -> p k c", p=128))
            s.inc(lw, "wl1", 16)
            s.wait(ten, "wl1")
            for tt in range(NKT2):
                tsl = slice(tt * 128, (tt + 1) * 128)
                s.wait(ten, "v")
                for kt in range(KT):
                    lastm = ten.matmul(pb[4][:, 0:256], xn[:, kt, tsl],
                                       wv_sb[:, kt, :], start=(kt == 0),
                                       stop=(kt == KT - 1))
                s.inc(lastm, "t")
                s.wait(vec, "t")
                vec.tensor_copy(vtm[:, tt, :], pb[4][:, 0:256])
            vload_mark = s.mark("t")
            for qc in range(NQC):
                qsl = slice(qc * 512, (qc + 1) * 512)
                s.wait(sync, "v")
                last = sync.dma_start(
                    out=cms[:, :, :],
                    in_=cm[:, qsl].rearrange("(kt p) q -> p kt q", p=128))
                s.inc(last, "eb", 16)
                s.wait(vec, "eb")
                for kt2 in range(NKT2):
                    s.wait(ten, "a")
                    lastm = ten.matmul(pb[0][:, :], kfm[:, kt2 * 128:(kt2 + 1) * 128],
                                       qfm[:, qsl], start=True, stop=True,
                                       skip_group_check=True)
                    s.inc(lastm, "t")
                    s.wait(act, "t")
                    s.wait(act, "v")
                    act.activation(osb[:, :], pb[0][:, :], AF.Exp, scale=0.125)
                    s.wait(vec, "a")
                    vec.tensor_mul(cms[:, kt2, :], osb[:, :], cms[:, kt2, :])
                    s.wait(ten, "v")
                    lastm = ten.matmul(pb[1][0:1, :], ones_bf[:, :],
                                       cms[:, kt2, :], start=(kt2 == 0),
                                       stop=(kt2 == NKT2 - 1),
                                       skip_group_check=True)
                    s.inc(lastm, "t")
                if vh == 0:
                    s.wait(vec, "t")
                    vec.reciprocal(rdq[0:1, qc, :], pb[1][0:1, :])
                for dvt in range(2):
                    s.wait(ten, "v")
                    for kt2 in range(NKT2):
                        lastm = ten.matmul(
                            pb[2][:, :], vtm[:, kt2, dvt * 128:(dvt + 1) * 128],
                            cms[:, kt2, :], start=(kt2 == 0),
                            stop=(kt2 == NKT2 - 1), skip_group_check=True)
                    s.inc(lastm, "t")
                    s.wait(ten, "v")
                    lastm = ten.matmul(pb[3][:, :], ones_f[:, :], rdq[0:1, qc, :],
                                       start=True, stop=True, skip_group_check=True)
                    s.inc(lastm, "t")
                    s.wait(vec, "t")
                    vec.tensor_copy(osb[:, :], pb[2][:, :])
                    ot = vh * 2 + dvt
                    s.wait(sync, "v")
                    last = sync.dma_start(out=xr[:, :],
                                          in_=xq[ot * 128:(ot + 1) * 128, qsl])
                    s.inc(last, "xd", 16)
                    s.wait(vec, "xd")
                    s.wait(vec, "st")
                    vec.tensor_mul(t1[:, :], osb[:, :], pb[3][:, :])
                    vec.tensor_add(t1[:, :], t1[:, :], xr[:, :])
                    s.wait(sync, "v")
                    last = sync.dma_start(out=y[ot * 128:(ot + 1) * 128, qsl],
                                          in_=t1[:, :])
                    s.inc(last, "st", 16)
                    s.wait(ten, "v")
        s.wait(sync, "st")
        s.wait(gps, "st")
    return nc


def build_attn_b1(S=4096, Q=2048):
    NKT2 = S // 128
    NQC = Q // 512
    p = P("attn_b1")
    nc = p.nc
    with p.ctx:
        xf = nc.dram_tensor("xf", [D, S], F32, kind="ExternalInput").ap()
        xq = nc.dram_tensor("xq", [D, Q], F32, kind="ExternalInput").ap()
        wkT = nc.dram_tensor("wkT", [D, D], F32, kind="ExternalInput").ap()
        wqT = nc.dram_tensor("wqT", [D, D], F32, kind="ExternalInput").ap()
        wvT = nc.dram_tensor("wvT", [D, D], F32, kind="ExternalInput").ap()
        woT = nc.dram_tensor("woT", [D, D], F32, kind="ExternalInput").ap()
        bo = nc.dram_tensor("bo", [D, 1], F32, kind="ExternalInput").ap()
        klng2 = nc.dram_tensor("klng2", [128, 1], F32, kind="ExternalInput").ap()
        klnb2 = nc.dram_tensor("klnb2", [128, 1], F32, kind="ExternalInput").ap()
        qlng2 = nc.dram_tensor("qlng2", [128, 1], F32, kind="ExternalInput").ap()
        qlnb2 = nc.dram_tensor("qlnb2", [128, 1], F32, kind="ExternalInput").ap()
        cm = nc.dram_tensor("cm", [S, Q], BF16, kind="ExternalInput").ap()
        blk2Tc = nc.dram_tensor("blk2Tc", [2, 128], F32, kind="ExternalInput").ap()
        y = nc.dram_tensor("y", [D, Q], F32, kind="ExternalOutput").ap()

        xn = p.sb("xn", [128, KT, S], BF16)
        qn_all = p.sb("qn_all", [128, KT, Q], BF16)
        xr = p.sb("xr", [128, 512], F32)
        xbf = p.sb("xbf", [128, 512], BF16)
        xsq = p.sb("xsq", [128, 512], BF16)
        stats = p.sb("stats", [1, 5, 512], F32)
        hstats = p.sb("hstats", [2, 5, 512], F32)
        t1 = p.sb("t1", [128, 512], F32)
        t2 = p.sb("t2", [128, 512], F32)
        ones_bf = p.sb("ones_bf", [128, 1], BF16)
        ones_f = p.sb("ones_f", [1, 128], F32)
        blk2 = p.sb("blk2", [128, 2], BF16)
        blk2T = p.sb("blk2T", [2, 128], BF16)
        gln = p.sb("gln", [128, 4], F32)
        wsb = [p.sb(f"wsb{i}", [128, KT, 128], BF16) for i in range(2)]
        kfm = p.sb("kfm", [128, S], BF16)
        vtm = p.sb("vtm", [128, NKT2, 130], BF16)
        ptile = p.sb("ptile", [128, 512], BF16)
        osb = p.sb("osb", [128, 512], BF16)
        rd1 = p.sb("rd1", [1, 512], F32)
        bsb = p.sb("bsb", [128, KT, 1], F32)

        pb = [p.ps(f"pb{i}", [128, 512]) for i in range(6)]

        s = p.engines(["xd", "wl0", "wl1", "eb", "st", "ld1"])
        sync, gps, ten = p.sync, p.gpsd, p.ten
        vec, act = p.vec, p.act

        vec.memset(ones_bf[:, :], 1.0)
        vec.memset(ones_f[:, :], 1.0)
        vec.memset(blk2[:, :], 0.0)
        vec.memset(blk2[0:64, 0:1], 1.0)
        vec.memset(blk2[64:128, 1:2], 1.0)
        lw = gps.dma_start(out=blk2T[:, :], in_=blk2Tc)
        s.inc(lw, "ld1", 16)
        s.wait(ten, "ld1")
        vec.memset(vtm[:, :, 64:65], 1.0)
        vec.memset(vtm[:, :, 129:130], 1.0)
        for i, t in enumerate([klng2, klnb2, qlng2, qlnb2]):
            last = sync.dma_start(out=gln[:, i:i + 1], in_=t)
            s.inc(last, "xd", 16)
        last = sync.dma_start(out=bsb[:, :, :],
                              in_=bo.rearrange("(k p) o -> p k o", p=128),
                              allow_slow_non_contiguous=True)
        s.inc(last, "xd", 16)

        wslot_mm = [0, 0]

        def load_w(src, cols_off, slot, ch):
            s.wait(gps, "t", wslot_mm[slot])
            lw = gps.dma_start(
                out=wsb[slot][:, :, :],
                in_=src[:, cols_off:cols_off + 128].rearrange(
                    "(k p) c -> p k c", p=128))
            s.inc(lw, ch, 16)

        def headln(dst, src_fm, ncols, gcol, bcol):
            for c in range(ncols // 512):
                sl = slice(c * 512, (c + 1) * 512)
                s.wait(vec, "t")
                vec.tensor_copy(xbf[:, :], src_fm[:, sl])
                vec.tensor_mul(xsq[:, :], xbf[:, :], xbf[:, :])
                s.wait(ten, "v")
                ten.matmul(pb[4][0:2, :], blk2[:, :], xbf[:, :],
                           start=True, stop=True, skip_group_check=True)
                lastm = ten.matmul(pb[5][0:2, :], blk2[:, :], xsq[:, :],
                                   start=True, stop=True, skip_group_check=True)
                s.inc(lastm, "t")
                s.wait(vec, "t")
                vec.tensor_scalar(hstats[0:2, 2, :], pb[4][0:2, :], 1.0 / 64, None,
                                  op0=OP.mult)
                vec.tensor_scalar(hstats[0:2, 1, :], pb[5][0:2, :], 1.0 / 64, None,
                                  op0=OP.mult)
                vec.tensor_mul(hstats[0:2, 3, :], hstats[0:2, 2, :], hstats[0:2, 2, :])
                vec.tensor_sub(hstats[0:2, 3, :], hstats[0:2, 1, :], hstats[0:2, 3, :])
                vec.tensor_scalar(hstats[0:2, 3, :], hstats[0:2, 3, :], EPS, None,
                                  op0=OP.add)
                s.wait(act, "v")
                act.activation(hstats[0:2, 4, :], hstats[0:2, 3, :], AF.Ln)
                act.activation(hstats[0:2, 4, :], hstats[0:2, 4, :], AF.Exp,
                               scale=-0.5)
                s.wait(vec, "a")
                vec.tensor_copy(xbf[0:2, 0:512], hstats[0:2, 2, :])
                vec.tensor_copy(xsq[0:2, 0:512], hstats[0:2, 4, :])
                s.wait(ten, "v")
                ten.matmul(pb[4][:, :], blk2T[:, :], xbf[0:2, 0:512],
                           start=True, stop=True, skip_group_check=True)
                lastm = ten.matmul(pb[5][:, :], blk2T[:, :], xsq[0:2, 0:512],
                                   start=True, stop=True, skip_group_check=True)
                s.inc(lastm, "t")
                s.wait(vec, "t")
                vec.tensor_sub(t2[:, :], src_fm[:, sl], pb[4][:, :])
                vec.tensor_mul(t2[:, :], t2[:, :], pb[5][:, :])
                vec.tensor_scalar(dst[:, sl], t2[:, :], gln[:, gcol:gcol + 1],
                                  gln[:, bcol:bcol + 1], op0=OP.mult, op1=OP.add)
                s.wait(ten, "v")

        from contextlib import ExitStack as _ES
        with _ES() as sub:
            xch = sub.enter_context(nc.sbuf_tensor("xch", [128, KT, 512], F32))
            xnq = sub.enter_context(nc.sbuf_tensor("xnq", [128, KT, Q], BF16))
            emit_ln(p, s, S, xf, xn, xch, xbf, xsq, stats, t1, ones_bf, ones_f,
                    pb[0][0:1, :], pb[3][0:1, :], pb[1], pb[2])
            emit_ln(p, s, Q, xq, xnq, xch, xbf, xsq, stats, t1, ones_bf, ones_f,
                    pb[0][0:1, :], pb[3][0:1, :], pb[1], pb[2])
            for hp in range(8):
                load_w(wqT, hp * 128, 1, "wl1")
                s.wait(ten, "wl1")
                for c in range(NQC):
                    sl = slice(c * 512, (c + 1) * 512)
                    s.wait(ten, "v")
                    for kt in range(KT):
                        lastm = ten.matmul(pb[5][:, :], wsb[1][:, kt, :],
                                           xnq[:, kt, sl], start=(kt == 0),
                                           stop=(kt == KT - 1))
                    s.inc(lastm, "t")
                    s.wait(vec, "t")
                    vec.tensor_copy(qn_all[:, hp, sl], pb[5][:, :])
                wslot_mm[1] = s.mark("t")
                headln(qn_all[:, hp, :], qn_all[:, hp, :], Q, 2, 3)
        cms = p.sb("cms", [128, 16, 512], BF16)
        ofm = p.sb("ofm", [128, KT, Q], BF16)

        for hp in range(8):
            load_w(wkT, hp * 128, 0, "wl0")
            s.wait(ten, "wl0")
            s.wait(ten, "a")
            s.wait(ten, "v")
            for c in range(S // 512):
                sl = slice(c * 512, (c + 1) * 512)
                s.wait(ten, "v")
                for kt in range(KT):
                    lastm = ten.matmul(pb[4][:, :], wsb[0][:, kt, :], xn[:, kt, sl],
                                       start=(kt == 0), stop=(kt == KT - 1))
                s.inc(lastm, "t")
                s.wait(vec, "t")
                vec.tensor_copy(kfm[:, sl], pb[4][:, :])
            wslot_mm[0] = s.mark("t")
            headln(kfm, kfm, S, 0, 1)
            load_w(wvT, hp * 128, 0, "wl0")
            s.wait(ten, "wl0")
            s.wait(ten, "v")
            for tt in range(NKT2):
                tsl = slice(tt * 128, (tt + 1) * 128)
                s.wait(ten, "v")
                for kt in range(KT):
                    lastm = ten.matmul(pb[4][:, 0:128], xn[:, kt, tsl],
                                       wsb[0][:, kt, :],
                                       start=(kt == 0), stop=(kt == KT - 1))
                s.inc(lastm, "t")
                s.wait(vec, "t")
                vec.tensor_copy(vtm[:, tt, 0:64], pb[4][:, 0:64])
                vec.tensor_copy(vtm[:, tt, 65:129], pb[4][:, 64:128])
            wslot_mm[0] = s.mark("t")
            s.wait(ten, "v")

            for qc in range(NQC):
                qsl = slice(qc * 512, (qc + 1) * 512)
                for kh in range(2):
                    s.wait(sync, "v")
                    last = sync.dma_start(
                        out=cms[:, :, :],
                        in_=cm[kh * 2048:(kh + 1) * 2048, qsl].rearrange(
                            "(kt p) q -> p kt q", p=128))
                    s.inc(last, "eb", 16)
                    s.wait(vec, "eb")
                    for h2 in range(2):
                        hb = 64 * h2
                        opb = pb[2 + h2]
                        for k16 in range(16):
                            kt2 = kh * 16 + k16
                            s.wait(ten, "a")
                            s.wait(ten, "v")
                            lastm = ten.matmul(
                                pb[0][:, :],
                                kfm[hb:hb + 64, kt2 * 128:(kt2 + 1) * 128],
                                qn_all[hb:hb + 64, hp, qsl],
                                start=True, stop=True, skip_group_check=True)
                            s.inc(lastm, "t")
                            s.wait(act, "t")
                            s.wait(act, "v")
                            act.activation(osb[:, :], pb[0][:, :],
                                           AF.Exp, scale=1.0 / 32.0)
                            s.wait(vec, "a")
                            vec.tensor_mul(ptile[:, :], osb[:, :], cms[:, k16, :])
                            s.wait(ten, "v")
                            lastm = ten.matmul(
                                opb[0:65, :],
                                vtm[:, kt2, 65 * h2:65 * h2 + 65],
                                ptile[:, :],
                                start=(kt2 == 0), stop=(kt2 == NKT2 - 1),
                                skip_group_check=True)
                            s.inc(lastm, "t")
                for h2 in range(2):
                    opb = pb[2 + h2]
                    s.wait(vec, "t")
                    vec.reciprocal(rd1[0:1, :], opb[64:65, :])
                    s.wait(ten, "v")
                    lastm = ten.matmul(pb[5][0:64, :], ones_f[:, 0:64],
                                       rd1[0:1, :], start=True, stop=True,
                                       skip_group_check=True)
                    s.inc(lastm, "t")
                    s.wait(vec, "t")
                    vec.tensor_copy(osb[0:64, :], opb[0:64, :])
                    h_global = hp * 2 + h2
                    otile = h_global // 2
                    orow = 64 * (h_global % 2)
                    vec.tensor_mul(ofm[orow:orow + 64, otile, qsl],
                                   osb[0:64, :], pb[5][0:64, :])
                    s.wait(ten, "v")

        wslot2 = list(wslot_mm)
        for ot in range(KT):
            slot = ot % 2
            ch = f"wl{slot}"
            s.wait(gps, "t", wslot2[slot])
            lw = gps.dma_start(
                out=wsb[slot][:, :, :],
                in_=woT[:, ot * 128:(ot + 1) * 128].rearrange(
                    "(k p) c -> p k c", p=128))
            s.inc(lw, ch, 16)
            s.wait(ten, ch)
            for c in range(NQC):
                sl = slice(c * 512, (c + 1) * 512)
                s.wait(ten, "v")
                for kt in range(KT):
                    lastm = ten.matmul(pb[4][:, :], wsb[slot][:, kt, :],
                                       ofm[:, kt, sl], start=(kt == 0),
                                       stop=(kt == KT - 1))
                s.inc(lastm, "t")
                s.wait(sync, "v")
                last = sync.dma_start(out=xr[:, :],
                                      in_=xq[ot * 128:(ot + 1) * 128, sl])
                s.inc(last, "xd", 16)
                s.wait(vec, "xd")
                s.wait(vec, "t")
                s.wait(vec, "st")
                vec.tensor_scalar(t1[:, :], pb[4][:, :], bsb[:, ot, 0:1], None,
                                  op0=OP.add)
                vec.tensor_add(t1[:, :], t1[:, :], xr[:, :])
                s.wait(sync, "v")
                last = sync.dma_start(out=y[ot * 128:(ot + 1) * 128, sl],
                                      in_=t1[:, :])
                s.inc(last, "st", 16)
            wslot2[slot] = s.mark("t")
        s.wait(sync, "st")
        s.wait(gps, "st")
    return nc


def build_attn_b0(S=4096, Q=2048, A=3584, W=512, TOPK=64):
    NKT2 = S // 128
    NA = A // 128
    NQC = Q // 512
    p = P("attn_b0")
    nc = p.nc
    with p.ctx:
        xf = nc.dram_tensor("xf", [D, S], F32, kind="ExternalInput").ap()
        xq = nc.dram_tensor("xq", [D, Q], F32, kind="ExternalInput").ap()
        wqw = nc.dram_tensor("wqw", [D, D], F32, kind="ExternalInput").ap()
        wkw = nc.dram_tensor("wkw", [D, D], F32, kind="ExternalInput").ap()
        wvw = nc.dram_tensor("wvw", [D, D], F32, kind="ExternalInput").ap()
        wow = nc.dram_tensor("wow", [D, D], F32, kind="ExternalInput").ap()
        wqb = nc.dram_tensor("wqb", [1, D], F32, kind="ExternalInput").ap()
        wkb = nc.dram_tensor("wkb", [1, D], F32, kind="ExternalInput").ap()
        wvb = nc.dram_tensor("wvb", [1, D], F32, kind="ExternalInput").ap()
        wob = nc.dram_tensor("wob", [D, 1], F32, kind="ExternalInput").ap()
        wqiw = nc.dram_tensor("wqiw", [D, 64], F32, kind="ExternalInput").ap()
        wkiw = nc.dram_tensor("wkiw", [D, 64], F32, kind="ExternalInput").ap()
        wqib = nc.dram_tensor("wqib", [64, 1], F32, kind="ExternalInput").ap()
        wkib = nc.dram_tensor("wkib", [64, 1], F32, kind="ExternalInput").ap()
        identf = nc.dram_tensor("identf", [128, 128], F32, kind="ExternalInput").ap()
        y = nc.dram_tensor("y", [D, Q], F32, kind="ExternalOutput").ap()
        odram = nc.dram_tensor("odram", [D, Q], BF16).ap()

        xn = p.sb("xn", [128, KT, S], BF16)
        xnq = p.sb("xnq", [128, KT, Q], BF16)
        xr = p.sb("xr", [128, 512], F32)
        xbf = p.sb("xbf", [128, 512], BF16)
        xsq = p.sb("xsq", [128, 512], BF16)
        stats = p.sb("stats", [1, 5, 512], F32)
        t1 = p.sb("t1", [128, 512], F32)
        ones_bf = p.sb("ones_bf", [128, 1], BF16)
        ones_f = p.sb("ones_f", [1, 128], F32)
        ident = p.sb("ident", [128, 128], F32)
        wsb = [p.sb(f"wsb{i}", [128, KT, 128], BF16) for i in range(2)]
        wisb = p.sb("wisb", [128, KT, 128], BF16)
        ibia = p.sb("ibia", [128, 2], F32)
        qidx = p.sb("qidx", [64, Q], BF16)
        kidx = p.sb("kidx", [64, A], BF16)
        thrr = p.sb("thrr", [1, Q], F32)
        thrb = p.sb("thrb", [128, 512], BF16)
        m8 = p.sb("m8", [128, 8], F32)
        bcv = p.sb("bcv", [128, 128], F32)
        kfm = p.sb("kfm", [128, S], BF16)
        qfm = p.sb("qfm", [128, Q], BF16)
        vtm = p.sb("vtm", [128, NKT2, 130], BF16)
        ptile = p.sb("ptile", [128, 512], BF16)
        osb = p.sb("osb", [128, 512], BF16)
        rd1 = p.sb("rd1", [1, 512], F32)
        bsb = p.sb("bsb", [128, KT, 1], F32)
        bqk = p.sb("bqk", [128, 2], F32)
        wvb_sb = p.sb("wvb_sb", [1, D], F32)

        pb = [p.ps(f"pb{i}", [128, 512]) for i in range(6)]

        s = p.engines(["xd", "wl0", "wl1", "eb", "st", "ld1"])
        sync, gps, ten = p.sync, p.gpsd, p.ten
        vec, act = p.vec, p.act

        vec.memset(ones_bf[:, :], 1.0)
        vec.memset(ones_f[:, :], 1.0)
        vec.memset(vtm[:, :, 64:65], 1.0)
        vec.memset(vtm[:, :, 129:130], 1.0)
        last = sync.dma_start(out=ident[:, :], in_=identf)
        s.inc(last, "xd", 16)
        last = sync.dma_start(out=bsb[:, :, :],
                              in_=wob.rearrange("(k p) o -> p k o", p=128),
                              allow_slow_non_contiguous=True)
        s.inc(last, "xd", 16)
        last = sync.dma_start(out=ibia[0:64, 0:1], in_=wqib)
        s.inc(last, "xd", 16)
        last = sync.dma_start(out=ibia[0:64, 1:2], in_=wkib)
        s.inc(last, "xd", 16)
        last = sync.dma_start(out=wvb_sb[:, :], in_=wvb)
        s.inc(last, "xd", 16)
        lw = gps.dma_start(out=wisb[:, :, 0:64],
                           in_=wqiw.rearrange("(k p) c -> p k c", p=128))
        s.inc(lw, "ld1", 16)
        lw = gps.dma_start(out=wisb[:, :, 64:128],
                           in_=wkiw.rearrange("(k p) c -> p k c", p=128))
        s.inc(lw, "ld1", 16)

        from contextlib import ExitStack as _ES
        with _ES() as sub:
            xch = sub.enter_context(nc.sbuf_tensor("xch", [128, KT, 512], F32))
            emit_ln(p, s, S, xf, xn, xch, xbf, xsq, stats, t1, ones_bf, ones_f,
                    pb[0][0:1, :], pb[3][0:1, :], pb[1], pb[2])
            emit_ln(p, s, Q, xq, xnq, xch, xbf, xsq, stats, t1, ones_bf, ones_f,
                    pb[0][0:1, :], pb[3][0:1, :], pb[1], pb[2])

        s.wait(ten, "ld1")
        s.wait(ten, "xd")
        s.wait(ten, "v")
        for c in range(NQC):
            sl = slice(c * 512, (c + 1) * 512)
            s.wait(ten, "v")
            for kt in range(KT):
                lastm = ten.matmul(pb[4][0:64, :], wisb[:, kt, 0:64],
                                   xnq[:, kt, sl], start=(kt == 0),
                                   stop=(kt == KT - 1))
            s.inc(lastm, "t")
            s.wait(vec, "t")
            vec.tensor_scalar(qidx[:, sl], pb[4][0:64, :], ibia[0:64, 0:1], None,
                              op0=OP.add)
        for c in range(A // 512):
            sl = slice(c * 512, (c + 1) * 512)
            s.wait(ten, "v")
            for kt in range(KT):
                lastm = ten.matmul(pb[4][0:64, :], wisb[:, kt, 64:128],
                                   xn[:, kt, sl], start=(kt == 0),
                                   stop=(kt == KT - 1))
            s.inc(lastm, "t")
            s.wait(vec, "t")
            vec.tensor_scalar(kidx[:, sl], pb[4][0:64, :], ibia[0:64, 1:2], None,
                              op0=OP.add)

        with _ES() as sub:
            scr = sub.enter_context(nc.sbuf_tensor("scr", [128, A], F32))
            for qt in range(Q // 128):
                tsl = slice(qt * 128, (qt + 1) * 128)
                for kc in range(A // 512):
                    ksl = slice(kc * 512, (kc + 1) * 512)
                    s.wait(ten, "a")
                    s.wait(ten, "v")
                    lastm = ten.matmul(pb[4][:, :], qidx[:, tsl], kidx[:, ksl],
                                       start=True, stop=True,
                                       skip_group_check=True)
                    s.inc(lastm, "t")
                    s.wait(act, "t")
                    s.wait(act, "v")
                    act.activation(scr[:, ksl], pb[4][:, :], AF.Relu)
                for r in range(TOPK // 8):
                    s.wait(vec, "a")
                    vec.max(out=m8[:, :], in_=scr[:, :])
                    if r < TOPK // 8 - 1:
                        vec.match_replace(out=scr[:, :], in_to_replace=m8[:, :],
                                          in_values=scr[:, :], imm_value=-1e30)
                s.wait(ten, "v")
                lastm = ten.transpose(pb[5][0:1, 0:128], m8[:, 7:8], ident[:, :])
                s.inc(lastm, "t")
                s.wait(vec, "t")
                vec.tensor_copy(thrr[0:1, tsl], pb[5][0:1, 0:128])
                s.wait(ten, "v")
                s.wait(act, "v")

        msks = p.sb("msks", [128, NA, 512], BF16)
        osb2 = p.sb("osb2", [64, 512], BF16)
        ofm_sb = p.sb("ofm_sb", [128, KT, 512], BF16)

        wslot_mm = [0, 0]

        def load_w(src, cols_off, slot, ch):
            s.wait(gps, "t", wslot_mm[slot])
            lw = gps.dma_start(
                out=wsb[slot][:, :, :],
                in_=src[:, cols_off:cols_off + 128].rearrange(
                    "(k p) c -> p k c", p=128))
            s.inc(lw, ch, 16)

        for hp in range(8):
            csl128 = slice(hp * 128, (hp + 1) * 128)
            s.wait(sync, "v")
            last = sync.dma_start(out=bqk[:, 0:1],
                                  in_=wqb[0:1, csl128].rearrange("o p -> p o"))
            s.inc(last, "xd", 16)
            last = sync.dma_start(out=bqk[:, 1:2],
                                  in_=wkb[0:1, csl128].rearrange("o p -> p o"))
            s.inc(last, "xd", 16)
            s.wait(ten, "xd")
            s.wait(ten, "v")
            lastm = ten.matmul(pb[5][:, 0:128], ones_f[:, :], wvb_sb[0:1, csl128],
                               start=True, stop=True, skip_group_check=True)
            s.inc(lastm, "t")
            s.wait(vec, "t")
            vec.tensor_copy(bcv[:, :], pb[5][:, 0:128])

            load_w(wkw, hp * 128, 0, "wl0")
            s.wait(ten, "wl0")
            s.wait(ten, "a")
            s.wait(ten, "v")
            for c in range(S // 512):
                sl = slice(c * 512, (c + 1) * 512)
                s.wait(ten, "v")
                for kt in range(KT):
                    lastm = ten.matmul(pb[4][:, :], wsb[0][:, kt, :], xn[:, kt, sl],
                                       start=(kt == 0), stop=(kt == KT - 1))
                s.inc(lastm, "t")
                s.wait(vec, "t")
                vec.tensor_scalar(kfm[:, sl], pb[4][:, :], bqk[:, 1:2], None,
                                  op0=OP.add)
            wslot_mm[0] = s.mark("t")
            load_w(wqw, hp * 128, 1, "wl1")
            s.wait(ten, "wl1")
            s.wait(ten, "v")
            for c in range(NQC):
                sl = slice(c * 512, (c + 1) * 512)
                s.wait(ten, "v")
                for kt in range(KT):
                    lastm = ten.matmul(pb[5][:, :], wsb[1][:, kt, :], xnq[:, kt, sl],
                                       start=(kt == 0), stop=(kt == KT - 1))
                s.inc(lastm, "t")
                s.wait(vec, "t")
                vec.tensor_scalar(qfm[:, sl], pb[5][:, :], bqk[:, 0:1], None,
                                  op0=OP.add)
            wslot_mm[1] = s.mark("t")
            load_w(wvw, hp * 128, 0, "wl0")
            s.wait(ten, "wl0")
            s.wait(ten, "v")
            for tt in range(NKT2):
                tsl = slice(tt * 128, (tt + 1) * 128)
                s.wait(ten, "v")
                for kt in range(KT):
                    lastm = ten.matmul(pb[4][:, 0:128], xn[:, kt, tsl],
                                       wsb[0][:, kt, :],
                                       start=(kt == 0), stop=(kt == KT - 1))
                s.inc(lastm, "t")
                s.wait(vec, "t")
                vec.tensor_add(t1[:, 0:128], pb[4][:, 0:128], bcv[:, :])
                vec.tensor_copy(vtm[:, tt, 0:64], t1[:, 0:64])
                vec.tensor_copy(vtm[:, tt, 65:129], t1[:, 64:128])
            wslot_mm[0] = s.mark("t")
            s.wait(ten, "v")

            for qc in range(NQC):
                qsl = slice(qc * 512, (qc + 1) * 512)
                s.wait(ten, "v")
                lastm = ten.matmul(pb[5][:, :], ones_f[:, :], thrr[0:1, qsl],
                                   start=True, stop=True, skip_group_check=True)
                s.inc(lastm, "t")
                s.wait(vec, "t")
                vec.tensor_copy(thrb[:, :], pb[5][:, :])
                for ka in range(NA):
                    s.wait(ten, "v")
                    lastm = ten.matmul(pb[5][:, :],
                                       kidx[:, ka * 128:(ka + 1) * 128],
                                       qidx[:, qsl], start=True, stop=True,
                                       skip_group_check=True)
                    s.inc(lastm, "t")
                    s.wait(vec, "t")
                    vec.tensor_tensor(msks[:, ka, :], pb[5][:, :], thrb[:, :],
                                      OP.is_ge)
                s.wait(ten, "v")
                for h2 in range(2):
                    hb = 64 * h2
                    opb = pb[2 + h2]
                    for kt2 in range(NKT2):
                        s.wait(ten, "a")
                        s.wait(ten, "v")
                        lastm = ten.matmul(
                            pb[0][:, :],
                            kfm[hb:hb + 64, kt2 * 128:(kt2 + 1) * 128],
                            qfm[hb:hb + 64, qsl],
                            start=True, stop=True, skip_group_check=True)
                        s.inc(lastm, "t")
                        s.wait(act, "t")
                        s.wait(act, "v")
                        act.activation(osb[:, :], pb[0][:, :], AF.Exp, scale=0.125)
                        s.wait(vec, "a")
                        if kt2 < NA:
                            vec.tensor_mul(ptile[:, :], osb[:, :], msks[:, kt2, :])
                        else:
                            vec.tensor_copy(ptile[:, :], osb[:, :])
                        s.wait(ten, "v")
                        lastm = ten.matmul(
                            opb[0:65, :],
                            vtm[:, kt2, 65 * h2:65 * h2 + 65],
                            ptile[:, :],
                            start=(kt2 == 0), stop=(kt2 == NKT2 - 1),
                            skip_group_check=True)
                        s.inc(lastm, "t")
                for h2 in range(2):
                    opb = pb[2 + h2]
                    s.wait(vec, "t")
                    vec.reciprocal(rd1[0:1, :], opb[64:65, :])
                    s.wait(ten, "v")
                    lastm = ten.matmul(pb[5][0:64, :], ones_f[:, 0:64],
                                       rd1[0:1, :], start=True, stop=True,
                                       skip_group_check=True)
                    s.inc(lastm, "t")
                    s.wait(vec, "t")
                    vec.tensor_copy(osb[0:64, :], opb[0:64, :])
                    h_global = hp * 2 + h2
                    orow = h_global * 64
                    s.wait(vec, "st")
                    vec.tensor_mul(osb2[0:64, :], osb[0:64, :], pb[5][0:64, :])
                    s.wait(sync, "v")
                    last = sync.dma_start(out=odram[orow:orow + 64, qsl],
                                          in_=osb2[0:64, :])
                    s.inc(last, "st", 16)
                    s.wait(ten, "v")

        s.wait(sync, "st")
        wslot2 = list(wslot_mm)
        for c in range(NQC):
            sl = slice(c * 512, (c + 1) * 512)
            s.wait(sync, "t", 0 if c == 0 else s.mark("t"))
            last = sync.dma_start(
                out=ofm_sb[:, :, :],
                in_=odram[:, sl].rearrange("(k p) q -> p k q", p=128))
            s.inc(last, "eb", 16)
            for ot in range(KT):
                slot = ot % 2
                ch = f"wl{slot}"
                s.wait(gps, "t", wslot2[slot])
                lw = gps.dma_start(
                    out=wsb[slot][:, :, :],
                    in_=wow[:, ot * 128:(ot + 1) * 128].rearrange(
                        "(k p) c -> p k c", p=128))
                s.inc(lw, ch, 16)
                s.wait(ten, ch)
                s.wait(ten, "eb")
                s.wait(ten, "v")
                for kt in range(KT):
                    lastm = ten.matmul(pb[4][:, :], wsb[slot][:, kt, :],
                                       ofm_sb[:, kt, :], start=(kt == 0),
                                       stop=(kt == KT - 1))
                s.inc(lastm, "t")
                wslot2[slot] = s.mark("t")
                s.wait(sync, "v")
                last = sync.dma_start(out=xr[:, :],
                                      in_=xq[ot * 128:(ot + 1) * 128, sl])
                s.inc(last, "xd", 16)
                s.wait(vec, "xd")
                s.wait(vec, "t")
                s.wait(vec, "st")
                vec.tensor_scalar(t1[:, :], pb[4][:, :], bsb[:, ot, 0:1], None,
                                  op0=OP.add)
                vec.tensor_add(t1[:, :], t1[:, :], xr[:, :])
                s.wait(sync, "v")
                last = sync.dma_start(out=y[ot * 128:(ot + 1) * 128, sl],
                                      in_=t1[:, :])
                s.inc(last, "st", 16)
        s.wait(sync, "st")
        s.wait(gps, "st")
    return nc


def build_gla(S=4096):
    NCH = S // 128
    p = P("gla")
    nc = p.nc
    H, DK = 8, 64
    with p.ctx:
        xf = nc.dram_tensor("xf", [D, S], F32, kind="ExternalInput").ap()
        wq = nc.dram_tensor("wq", [D, 512], F32, kind="ExternalInput").ap()
        wk = nc.dram_tensor("wk", [D, 512], F32, kind="ExternalInput").ap()
        wv = nc.dram_tensor("wv", [D, 512], F32, kind="ExternalInput").ap()
        wqa = nc.dram_tensor("wqa", [D, 64], F32, kind="ExternalInput").ap()
        wka = nc.dram_tensor("wka", [D, 64], F32, kind="ExternalInput").ap()
        wga = nc.dram_tensor("wga", [D, 64], F32, kind="ExternalInput").ap()
        wqb = nc.dram_tensor("wqb", [64, 512], F32, kind="ExternalInput").ap()
        wkb = nc.dram_tensor("wkb", [64, 512], F32, kind="ExternalInput").ap()
        wgb = nc.dram_tensor("wgb", [64, 512], F32, kind="ExternalInput").ap()
        wg1a = nc.dram_tensor("wg1a", [D, 16], F32, kind="ExternalInput").ap()
        wg2a = nc.dram_tensor("wg2a", [D, 16], F32, kind="ExternalInput").ap()
        wg1b = nc.dram_tensor("wg1b", [16, 512], F32, kind="ExternalInput").ap()
        wg2b = nc.dram_tensor("wg2b", [16, 512], F32, kind="ExternalInput").ap()
        bg1r = nc.dram_tensor("bg1r", [512, 1], F32, kind="ExternalInput").ap()
        bg2r = nc.dram_tensor("bg2r", [512, 1], F32, kind="ExternalInput").ap()
        we = nc.dram_tensor("we", [D, 4], F32, kind="ExternalInput").ap()
        normwb = nc.dram_tensor("normwb", [1, 512], F32, kind="ExternalInput").ap()
        wo = nc.dram_tensor("wo", [512, D], F32, kind="ExternalInput").ap()
        trilm = nc.dram_tensor("trilm", [128, 128], BF16, kind="ExternalInput").ap()
        identf = nc.dram_tensor("identf", [128, 128], F32, kind="ExternalInput").ap()
        blk2Tc = nc.dram_tensor("blk2Tc", [2, 128], F32, kind="ExternalInput").ap()
        y = nc.dram_tensor("y", [D, S], F32, kind="ExternalOutput").ap()

        xn = p.sb("xn", [128, KT, S], BF16)
        xch = p.sb("xch", [128, KT, 512], F32)
        xr = p.sb("xr", [128, 128], F32)
        xbf = p.sb("xbf", [128, 512], BF16)
        xsq = p.sb("xsq", [128, 512], BF16)
        stats = p.sb("stats", [1, 5, 512], F32)
        t1 = p.sb("t1", [128, 512], F32)
        t2 = p.sb("t2", [128, 512], F32)
        tb = p.sb("tb", [128, 512], BF16)
        ones_bf = p.sb("ones_bf", [128, 1], BF16)
        ones_f = p.sb("ones_f", [1, 128], F32)
        ident = p.sb("ident", [128, 128], F32)
        identb = p.sb("identb", [128, 128], BF16)
        tril = p.sb("tril", [128, 128], BF16)
        nwb0 = p.sb("nwb0", [128, 512], F32)
        dec = p.sb("dec", [128, 4], F32)
        zer = p.sb("zer", [128, 128], F32)
        blk2 = p.sb("blk2", [128, 2], BF16)
        blk2T = p.sb("blk2T", [2, 128], BF16)
        nwb = p.sb("nwb", [1, 512], F32)
        wq_s = p.sb("wq_s", [128, KT, 512], BF16)
        wk_s = p.sb("wk_s", [128, KT, 512], BF16)
        wv_s = p.sb("wv_s", [128, KT, 512], BF16)
        wqa_s = p.sb("wqa_s", [128, KT, 64], BF16)
        wka_s = p.sb("wka_s", [128, KT, 64], BF16)
        wga_s = p.sb("wga_s", [128, KT, 64], BF16)
        wqb_s = p.sb("wqb_s", [64, 512], BF16)
        wkb_s = p.sb("wkb_s", [64, 512], BF16)
        wgb_s = p.sb("wgb_s", [64, 512], BF16)
        wg1a_s = p.sb("wg1a_s", [128, KT, 16], BF16)
        wg2a_s = p.sb("wg2a_s", [128, KT, 16], BF16)
        wg1b_s = p.sb("wg1b_s", [16, 512], BF16)
        wg2b_s = p.sb("wg2b_s", [16, 512], BF16)
        bg1_s = p.sb("bg1_s", [128, 4], F32)
        bg2_s = p.sb("bg2_s", [128, 4], F32)
        we_s = p.sb("we_s", [128, KT, 4], BF16)
        wo_s = p.sb("wo_s", [128, 4, D], BF16)
        q1c = p.sb("q1c", [128, 4, 128], BF16)
        k1c = p.sb("k1c", [128, 4, 128], BF16)
        q2c = p.sb("q2c", [128, 4, 128], BF16)
        k2c = p.sb("k2c", [128, 4, 128], BF16)
        vc = p.sb("vc", [128, 512], BF16)
        gk1c = p.sb("gk1c", [128, 4, 128], F32)
        gk2c = p.sb("gk2c", [128, 4, 128], F32)
        gke = p.sb("gke", [128, 4, 128], F32)
        egv = p.sb("egv", [128, 4, 128], F32)
        env = p.sb("env", [128, 4, 128], F32)
        elv = p.sb("elv", [128, 4, 128], F32)
        qt_ = p.sb("qt_", [128, 4, 128], BF16)
        kg_ = p.sb("kg_", [128, 4, 128], BF16)
        ko_ = p.sb("ko_", [128, 4, 128], BF16)
        kot = p.sb("kot", [128, 512], BF16)
        pa_ = p.sb("pa_", [128, 8, 128], BF16)
        facb = p.sb("facb", [128, 128], BF16)
        mkb = p.sb("mkb", [128, 128], BF16)
        xl_ = p.sb("xl_", [64, 128], BF16)
        glc = p.sb("glc", [128, 512], BF16)
        rwt = p.sb("rwt", [128, 8], F32)
        ofm = p.sb("ofm", [128, 4, 128], BF16)
        S_ = [p.sb(f"S{i}", [128, 4, 64], F32) for i in range(5)]
        Sbf = p.sb("Sbf", [128, 4, 64], BF16)

        pb = [p.ps(f"pb{i}", [128, 512]) for i in range(6)]
        pbt = p.ps("pbt", [128, 512], BF16)

        s = p.engines(["xd", "wl0", "st", "ld1"])
        sync, gps, ten = p.sync, p.gpsd, p.ten
        vec, act = p.vec, p.act

        vec.memset(ones_bf[:, :], 1.0)
        vec.memset(ones_f[:, :], 1.0)
        vec.memset(blk2[:, :], 0.0)
        vec.memset(blk2[0:64, 0:1], 1.0)
        vec.memset(blk2[64:128, 1:2], 1.0)
        for i in range(5):
            vec.memset(S_[i][:, :, :], 0.0)
        vec.memset(Sbf[:, :, :], 0.0)

        def ld(dst, src, rearr=None, ch="ld1"):
            inp = src if rearr is None else src.rearrange(rearr, p=128)
            lw = gps.dma_start(out=dst, in_=inp)
            s.inc(lw, ch, 16)

        ld(wq_s[:, :, :], wq, "(k p) c -> p k c")
        ld(wk_s[:, :, :], wk, "(k p) c -> p k c")
        ld(wv_s[:, :, :], wv, "(k p) c -> p k c")
        ld(wqa_s[:, :, :], wqa, "(k p) c -> p k c")
        ld(wka_s[:, :, :], wka, "(k p) c -> p k c")
        ld(wga_s[:, :, :], wga, "(k p) c -> p k c")
        ld(wqb_s[:, :], wqb)
        ld(wkb_s[:, :], wkb)
        ld(wgb_s[:, :], wgb)
        ld(wg1a_s[:, :, :], wg1a, "(k p) c -> p k c")
        ld(wg2a_s[:, :, :], wg2a, "(k p) c -> p k c")
        ld(wg1b_s[:, :], wg1b)
        ld(wg2b_s[:, :], wg2b)
        ld(we_s[:, :, :], we, "(k p) c -> p k c")
        ld(wo_s[:, :, :], wo, "(k p) c -> p k c")
        ld(tril[:, :], trilm)
        ld(blk2T[:, :], blk2Tc)
        lw = sync.dma_start(out=bg1_s[:, :], in_=bg1r.rearrange("(k p) o -> p k o", p=128),
                            allow_slow_non_contiguous=True)
        s.inc(lw, "xd", 16)
        lw = sync.dma_start(out=bg2_s[:, :], in_=bg2r.rearrange("(k p) o -> p k o", p=128),
                            allow_slow_non_contiguous=True)
        s.inc(lw, "xd", 16)
        lw = sync.dma_start(out=ident[:, :], in_=identf)
        s.inc(lw, "xd", 16)
        lw = sync.dma_start(out=nwb[:, :], in_=normwb)
        s.inc(lw, "xd", 16)

        emit_ln(p, s, S, xf, xn, xch, xbf, xsq, stats, t1, ones_bf, ones_f,
                pb[0][0:1, :], pb[3][0:1, :], pb[1], pb[2])

        s.wait(ten, "ld1")
        s.wait(ten, "xd")
        s.wait(ten, "v")
        s.wait(vec, "xd")
        vec.tensor_copy(identb[:, :], ident[:, :])
        vec.memset(zer[:, :], 0.0)
        lastm = ten.matmul(pb[5][:, :], ones_f[:, :], nwb[0:1, :],
                           start=True, stop=True)
        s.inc(lastm, "t")
        s.wait(vec, "t")
        vec.tensor_copy(nwb0[:, :], pb[5][:, :])
        s.wait(ten, "v")

        def proj_fm(dst, w_s, csl, ncol=512, scale=None, bias=None, post=None):
            for rt in range(ncol // 128):
                s.wait(ten, "v")
                for kt in range(KT):
                    lastm = ten.matmul(pb[0][:, 0:128], w_s[:, kt, rt * 128:(rt + 1) * 128],
                                       xn[:, kt, csl], start=(kt == 0),
                                       stop=(kt == KT - 1))
                s.inc(lastm, "t")
                s.wait(vec, "t")
                if scale is not None:
                    vec.tensor_scalar(dst[:, rt, :], pb[0][:, 0:128], scale, None,
                                      op0=OP.mult)
                else:
                    vec.tensor_copy(dst[:, rt, :], pb[0][:, 0:128])

        def logsig16(dst, raw_psum4, bias_s):
            for rt in range(4):
                s.wait(vec, "t")
                s.wait(vec, "st")
                vec.tensor_scalar(t1[:, 0:128], raw_psum4[rt], bias_s[:, rt:rt + 1],
                                  None, op0=OP.add)
                s.wait(act, "v")
                act.activation(t2[:, 0:128], t1[:, 0:128], AF.Abs)
                act.activation(t2[:, 0:128], t2[:, 0:128], AF.Exp, scale=-1.0)
                s.wait(vec, "a")
                vec.tensor_scalar(t2[:, 0:128], t2[:, 0:128], 1.0, None, op0=OP.add)
                s.wait(act, "v")
                act.activation(t2[:, 0:128], t2[:, 0:128], AF.Ln)
                s.wait(vec, "a")
                vec.tensor_scalar_min(t1[:, 0:128], t1[:, 0:128], 0.0)
                vec.tensor_sub(t1[:, 0:128], t1[:, 0:128], t2[:, 0:128])
                vec.tensor_scalar(dst[:, rt, :], t1[:, 0:128], 1.0 / 16.0, None,
                                  op0=OP.mult)

        for cc in range(NCH):
            csl = slice(cc * 128, (cc + 1) * 128)
            proj_fm(q1c, wq_s, csl, scale=0.125)
            proj_fm(k1c, wk_s, csl)
            s.wait(ten, "v")
            for kt in range(KT):
                lastm = ten.matmul(pb[0][:, :], xn[:, kt, csl], wv_s[:, kt, :],
                                   start=(kt == 0), stop=(kt == KT - 1))
            s.inc(lastm, "t")
            s.wait(vec, "t")
            vec.tensor_copy(vc[:, :], pb[0][:, :])
            s.wait(ten, "v")
            for kt in range(KT):
                lastm = ten.matmul(pb[0][0:64, 0:128], wqa_s[:, kt, :], xn[:, kt, csl],
                                   start=(kt == 0), stop=(kt == KT - 1))
            s.inc(lastm, "t")
            s.wait(vec, "t")
            vec.tensor_copy(xl_[:, :], pb[0][0:64, 0:128])
            s.wait(ten, "v")
            for rt in range(4):
                lastm = ten.matmul(pb[0][:, 0:128] if rt == 0 else pb[0][:, 0:128],
                                   wqb_s[:, rt * 128:(rt + 1) * 128], xl_[:, :],
                                   start=True, stop=True, skip_group_check=True)
                s.inc(lastm, "t")
                s.wait(vec, "t")
                vec.tensor_scalar(q2c[:, rt, :], pb[0][:, 0:128], 0.125, None,
                                  op0=OP.mult)
                s.wait(ten, "v")
            s.wait(ten, "v")
            for kt in range(KT):
                lastm = ten.matmul(pb[0][0:64, 0:128], wka_s[:, kt, :], xn[:, kt, csl],
                                   start=(kt == 0), stop=(kt == KT - 1))
            s.inc(lastm, "t")
            s.wait(vec, "t")
            vec.tensor_copy(xl_[:, :], pb[0][0:64, 0:128])
            s.wait(ten, "v")
            for rt in range(4):
                lastm = ten.matmul(pb[0][:, 0:128], wkb_s[:, rt * 128:(rt + 1) * 128],
                                   xl_[:, :], start=True, stop=True,
                                   skip_group_check=True)
                s.inc(lastm, "t")
                s.wait(act, "t")
                s.wait(act, "v")
                s.wait(act, "st")
                act.activation(t1[:, 0:128], pb[0][:, 0:128], AF.Exp)
                s.wait(vec, "a")
                vec.tensor_copy(tb[:, 0:128], t1[:, 0:128])
                s.wait(ten, "v")
                lastm = ten.matmul(pb[5][0:2, 0:128], blk2[:, :], tb[:, 0:128],
                                   start=True, stop=True, skip_group_check=True)
                s.inc(lastm, "t")
                s.wait(vec, "t")
                vec.reciprocal(t2[0:2, 0:128], pb[5][0:2, 0:128])
                vec.tensor_copy(xsq[0:2, 0:128], t2[0:2, 0:128])
                s.wait(ten, "v")
                lastm = ten.matmul(pb[5][:, 0:128], blk2T[:, :], xsq[0:2, 0:128],
                                   start=True, stop=True, skip_group_check=True)
                s.inc(lastm, "t")
                s.wait(vec, "t")
                vec.tensor_mul(k2c[:, rt, :], t1[:, 0:128], pb[5][:, 0:128])
                s.wait(ten, "v")
                s.wait(act, "v")
            for (ga_s, gb_s, bgs, dst) in ((wg1a_s, wg1b_s, bg1_s, gk1c),
                                           (wg2a_s, wg2b_s, bg2_s, gk2c)):
                s.wait(ten, "v")
                s.wait(ten, "a")
                for kt in range(KT):
                    lastm = ten.matmul(pb[0][0:16, 0:128], ga_s[:, kt, :],
                                       xn[:, kt, csl], start=(kt == 0),
                                       stop=(kt == KT - 1))
                s.inc(lastm, "t")
                s.wait(vec, "t")
                vec.tensor_copy(xl_[0:16, :], pb[0][0:16, 0:128])
                s.wait(ten, "v")
                for rt in range(4):
                    lastm = ten.matmul(pb[1][:, rt * 128:(rt + 1) * 128],
                                       gb_s[:, rt * 128:(rt + 1) * 128],
                                       xl_[0:16, :], start=(rt == 0), stop=(rt == 3),
                                       skip_group_check=True)
                    s.inc(lastm, "t")
                    s.wait(ten, "t")
                s.wait(vec, "t")
                logsig16(dst, [pb[1][:, rt * 128:(rt + 1) * 128] for rt in range(4)],
                         bgs)
            s.wait(ten, "v")
            s.wait(ten, "a")
            for kt in range(KT):
                lastm = ten.matmul(pb[0][0:64, 0:128], wga_s[:, kt, :], xn[:, kt, csl],
                                   start=(kt == 0), stop=(kt == KT - 1))
            s.inc(lastm, "t")
            s.wait(vec, "t")
            vec.tensor_copy(xl_[:, :], pb[0][0:64, 0:128])
            s.wait(ten, "v")
            lastm = ten.matmul(pb[0][:, :], xl_[:, :], wgb_s[:, :],
                               start=True, stop=True, skip_group_check=True)
            s.inc(lastm, "t")
            s.wait(act, "t")
            s.wait(act, "v")
            act.activation(glc[:, :], pb[0][:, :], AF.Silu)
            s.wait(ten, "a")
            s.wait(ten, "v")
            for kt in range(KT):
                lastm = ten.matmul(pb[0][:, 0:4], xn[:, kt, csl], we_s[:, kt, :],
                                   start=(kt == 0), stop=(kt == KT - 1))
            s.inc(lastm, "t")
            s.wait(act, "t")
            act.activation(t1[:, 0:4], pb[0][:, 0:4], AF.Exp)
            s.wait(vec, "a")
            vec.reduce_sum(t1[:, 4:5], t1[:, 0:4], axis=AX.X)
            vec.reciprocal(t1[:, 5:6], t1[:, 4:5])
            vec.tensor_scalar(t1[:, 0:4], t1[:, 0:4], t1[:, 5:6], None, op0=OP.mult)
            vec.reduce_max(t1[:, 6:7], t1[:, 0:4], axis=AX.X)
            vec.tensor_scalar(rwt[:, 4:8], t1[:, 0:4], t1[:, 6:7], None, op0=OP.is_ge)
            vec.tensor_mul(rwt[:, 0:4], t1[:, 0:4], rwt[:, 4:8])

            for inst in range(5):
                if inst == 0:
                    qsrc, ksrc, gsrc = q1c, k1c, gk1c
                else:
                    e = inst - 1
                    s.wait(ten, "v")
                    lastm = ten.transpose(pb[5][0:1, 0:128], rwt[:, e:e + 1],
                                          ident[:, :])
                    s.inc(lastm, "t")
                    s.wait(vec, "t")
                    vec.tensor_copy(t2[0:1, 0:128], pb[5][0:1, 0:128])
                    s.wait(ten, "v")
                    lastm = ten.transpose(pb[5][0:1, 128:256], rwt[:, 4 + e:5 + e],
                                          ident[:, :])
                    s.inc(lastm, "t")
                    s.wait(vec, "t")
                    vec.tensor_copy(t2[0:1, 128:256], pb[5][0:1, 128:256])
                    s.wait(ten, "v")
                    lastm = ten.matmul(pb[5][:, 0:128], ones_f[:, :],
                                       t2[0:1, 0:128], start=True, stop=False,
                                       skip_group_check=True)
                    s.inc(lastm, "t")
                    s.wait(ten, "t")
                    lastm = ten.matmul(pb[5][:, 128:256], ones_f[:, :],
                                       t2[0:1, 128:256], start=False, stop=True,
                                       skip_group_check=True)
                    s.inc(lastm, "t")
                    s.wait(ten, "t")
                    s.wait(vec, "t")
                    vec.tensor_copy(facb[:, :], pb[5][:, 0:128])
                    vec.tensor_copy(mkb[:, :], pb[5][:, 128:256])
                    s.wait(ten, "v")
                    qsrc, ksrc, gsrc = q2c, k2c, None
                if inst == 0:
                    gin = gk1c
                else:
                    vec.tensor_mul(gke[:, :, :], gk2c[:, :, :],
                                   mkb[:, :].unsqueeze(1).to_broadcast([128, 4, 128]))
                    gin = gke
                for rt in range(4):
                    vec.tensor_tensor_scan(gke[:, rt, :] if inst == 0 else egv[:, rt, :],
                                           zer[:, :], gin[:, rt, :], 0.0,
                                           op0=OP.add, op1=OP.add)
                gcs = gke if inst == 0 else egv
                for rt in range(4):
                    s.wait(act, "v")
                    act.activation(env[:, rt, :], gcs[:, rt, :], AF.Exp, scale=-1.0)
                    act.activation(dec[:, rt:rt + 1], gcs[:, rt, 127:128], AF.Exp)
                    s.wait(vec, "a")
                    vec.tensor_scalar(elv[:, rt, :], gcs[:, rt, :],
                                      gcs[:, rt, 127:128], None, op0=OP.subtract)
                    s.wait(act, "v")
                    act.activation(elv[:, rt, :], elv[:, rt, :], AF.Exp, scale=-1.0)
                    act.activation(egv[:, rt, :] if inst == 0 else t1[:, 0:128],
                                   gcs[:, rt, :], AF.Exp)
                    if inst != 0:
                        s.wait(vec, "a")
                        vec.tensor_copy(gke[:, rt, :], t1[:, 0:128])
                ege = egv if inst == 0 else gke
                s.wait(vec, "t")
                s.wait(vec, "a")
                vec.tensor_copy(Sbf[:, :, :], S_[inst][:, :, :])
                s.wait(vec, "a")
                if inst == 0:
                    vec.tensor_mul(qt_[:, :, :], qsrc[:, :, :], ege[:, :, :])
                    vec.tensor_mul(kg_[:, :, :], ksrc[:, :, :], env[:, :, :])
                    vec.tensor_mul(ko_[:, :, :], ksrc[:, :, :], elv[:, :, :])
                else:
                    fb = facb[:, :].unsqueeze(1).to_broadcast([128, 4, 128])
                    vec.tensor_mul(qt_[:, :, :], qsrc[:, :, :], ege[:, :, :])
                    vec.tensor_mul(qt_[:, :, :], qt_[:, :, :], fb)
                    vec.tensor_mul(kg_[:, :, :], ksrc[:, :, :], env[:, :, :])
                    vec.tensor_mul(kg_[:, :, :], kg_[:, :, :], fb)
                    vec.tensor_mul(ko_[:, :, :], ksrc[:, :, :], elv[:, :, :])
                    vec.tensor_mul(ko_[:, :, :], ko_[:, :, :], fb)
                s.wait(ten, "v")
                for h in range(H):
                    rt, rr = h // 2, 64 * (h % 2)
                    bank = pb[1] if h < 4 else pb[2]
                    lastm = ten.matmul(bank[:, (h % 4) * 128:(h % 4 + 1) * 128],
                                       kg_[rr:rr + 64, rt, :], qt_[rr:rr + 64, rt, :],
                                       start=(h % 4 == 0), stop=(h % 4 == 3),
                                       skip_group_check=True)
                    s.inc(lastm, "t")
                    s.wait(ten, "t")
                s.wait(vec, "t")
                vec.tensor_mul(pa_[:, 0:4, :], pb[1][:, :].rearrange("p (h t) -> p h t", h=4),
                               tril[:, :].unsqueeze(1).to_broadcast([128, 4, 128]))
                vec.tensor_mul(pa_[:, 4:8, :], pb[2][:, :].rearrange("p (h t) -> p h t", h=4),
                               tril[:, :].unsqueeze(1).to_broadcast([128, 4, 128]))
                for rt in range(4):
                    s.wait(ten, "v")
                    lastm = ten.transpose(pbt[:, rt * 128:(rt + 1) * 128],
                                          ko_[:, rt, :], identb[:, :])
                    s.inc(lastm, "t")
                    s.wait(vec, "t")
                    vec.tensor_copy(kot[:, rt * 128:(rt + 1) * 128],
                                    pbt[:, rt * 128:(rt + 1) * 128])
                s.wait(ten, "v")
                for h in range(H):
                    lastm = ten.matmul(pb[3][:, h * 64:(h + 1) * 64],
                                       pa_[:, h, :], vc[:, h * 64:(h + 1) * 64],
                                       start=(inst == 0 and h == 0), stop=False,
                                       skip_group_check=True)
                    s.inc(lastm, "t")
                    s.wait(ten, "t")
                for h in range(H):
                    rt, rr = h // 2, 64 * (h % 2)
                    lastm = ten.matmul(pb[3][:, h * 64:(h + 1) * 64],
                                       qt_[rr:rr + 64, rt, :], Sbf[rr:rr + 64, rt, :],
                                       start=False,
                                       stop=(inst == 4 and h == H - 1),
                                       skip_group_check=True)
                    s.inc(lastm, "t")
                    s.wait(ten, "t")
                s.wait(ten, "v")
                for h in range(H):
                    rt, rr = h // 2, 64 * (h % 2)
                    lastm = ten.matmul(pb[4][rr:rr + 64, rt * 64:(rt + 1) * 64],
                                       kot[:, h * 64:(h + 1) * 64],
                                       vc[:, h * 64:(h + 1) * 64],
                                       start=(h < 2), stop=(h >= H - 2),
                                       skip_group_check=True)
                    s.inc(lastm, "t")
                    s.wait(ten, "t")
                s.wait(vec, "t")
                for rt in range(4):
                    vec.scalar_tensor_tensor(
                        S_[inst][:, rt, :], S_[inst][:, rt, :],
                        dec[:, rt:rt + 1], pb[4][:, rt * 64:(rt + 1) * 64],
                        op0=OP.mult, op1=OP.add)
                s.wait(ten, "v")

            s.wait(vec, "t")
            vec.tensor_copy(t1[:, :], pb[3][:, :])
            vec.tensor_mul(t2[:, :], t1[:, :], t1[:, :])
            vec.reduce_sum(t2[:, 0:8],
                           t2[:, :].rearrange("p (h d) -> p h d", h=8), axis=AX.X)
            vec.tensor_scalar(t2[:, 0:8], t2[:, 0:8], 1.0 / 64, None, op0=OP.mult)
            vec.tensor_scalar(t2[:, 0:8], t2[:, 0:8], EPS, None, op0=OP.add)
            s.wait(act, "v")
            act.activation(t2[:, 8:16], t2[:, 0:8], AF.Ln)
            act.activation(t2[:, 8:16], t2[:, 8:16], AF.Exp, scale=-0.5)
            s.wait(vec, "a")
            vec.tensor_mul(t1[:, :].rearrange("p (h d) -> p h d", h=8),
                           t1[:, :].rearrange("p (h d) -> p h d", h=8),
                           t2[:, 8:16].rearrange("p (h o) -> p h o", h=8)
                           .to_broadcast([128, 8, 64]))
            vec.tensor_mul(t1[:, :], t1[:, :], nwb0[:, :])
            vec.tensor_mul(tb[:, :], t1[:, :], glc[:, :])
            for rt in range(4):
                s.wait(ten, "v")
                lastm = ten.transpose(pbt[:, rt * 128:(rt + 1) * 128],
                                      tb[:, rt * 128:(rt + 1) * 128], identb[:, :])
                s.inc(lastm, "t")
                s.wait(vec, "t")
                vec.tensor_copy(ofm[:, rt, :], pbt[:, rt * 128:(rt + 1) * 128])
            for ot in range(KT):
                s.wait(ten, "v")
                for rt in range(4):
                    lastm = ten.matmul(pb[0][:, 0:128],
                                       wo_s[:, rt, ot * 128:(ot + 1) * 128],
                                       ofm[:, rt, :], start=(rt == 0),
                                       stop=(rt == 3), skip_group_check=True)
                s.inc(lastm, "t")
                s.wait(sync, "v")
                last = sync.dma_start(out=xr[:, :],
                                      in_=xf[ot * 128:(ot + 1) * 128, csl])
                s.inc(last, "xd", 16)
                s.wait(vec, "xd")
                s.wait(vec, "t")
                s.wait(vec, "st")
                vec.tensor_add(t1[:, 0:128], pb[0][:, 0:128], xr[:, :])
                s.wait(sync, "v")
                last = sync.dma_start(out=y[ot * 128:(ot + 1) * 128, csl],
                                      in_=t1[:, 0:128])
                s.inc(last, "st", 16)
        s.wait(sync, "st")
        s.wait(gps, "st")
    return nc


import ml_dtypes

_CACHE = {}


def _prog(name, builder):
    if name not in _CACHE:
        _CACHE[name] = builder()
    return _CACHE[name]


def _exb_full():
    PHI = (1.0 + np.sqrt(5.0)) / 2.0
    Sn = 4096
    pos = (np.arange(Sn, dtype=np.float64) * PHI) % 1.0
    dist = np.abs(pos[:, None] - pos[None, :])
    return (np.exp(-dist) + 1e-8).astype(np.float32)


def kernel(**inp):
    inp = {k: np.asarray(v) for k, v in inp.items()}
    x0 = inp["L0_attn_a_x"].astype(np.float32)
    B, Sn, Dm = x0.shape
    Q = 2048
    cores = list(range(8))
    sh = [np.ascontiguousarray(x0[c // 2, (c % 2) * Q:(c % 2 + 1) * Q].T)
          for c in cores]

    def full_xf(c):
        b = c // 2
        return np.concatenate([sh[2 * b], sh[2 * b + 1]], axis=1)

    ident = np.eye(128, dtype=np.float32)
    blk2Tc = np.kron(np.eye(2), np.ones((1, 64))).astype(np.float32)
    trilc = (np.arange(128)[None, :] >= np.arange(128)[:, None])
    trilc = trilc.astype(ml_dtypes.bfloat16)
    cm_full = (np.arange(Sn)[:, None] <= np.arange(Sn)[None, :])

    wmap_a0 = {
        "wqkv": np.ascontiguousarray(inp["L0_attn_a_w_qkv"]).astype(np.float32),
        "wproj": np.ascontiguousarray(inp["L0_attn_a_w_proj"]).astype(np.float32),
        "bproj": inp["L0_attn_a_b_proj"].reshape(D, 1).astype(np.float32),
    }
    exbf = _exb_full()
    nc = _prog("a0", build_attn_a)
    im = []
    for c in cores:
        half = c % 2
        own = np.arange(half * Q, (half + 1) * Q)
        par = np.arange((1 - half) * Q, (2 - half) * Q)
        perm = np.r_[own, par]
        xf = np.concatenate([sh[c], sh[2 * (c // 2) + (1 - half)]], axis=1)
        exb_c = np.ascontiguousarray(exbf[np.ix_(perm, own)]).astype(ml_dtypes.bfloat16)
        m = {"xf": xf, "exb": exb_c}
        m.update(wmap_a0)
        im.append(m)
    res = run_bass_kernel_spmd(nc, im, core_ids=cores)
    sh = [np.ascontiguousarray(res.results[c]["y"]) for c in cores]

    wmap_f = host_moe_inputs(inp, "L0_ffn")
    nc = _prog("moe", build_moe)
    im = [dict(x=sh[c], **wmap_f) for c in cores]
    res = run_bass_kernel_spmd(nc, im, core_ids=cores)
    sh = [np.ascontiguousarray(res.results[c]["y"]) for c in cores]

    wmap_b0 = {
        "wqw": np.ascontiguousarray(inp["L0_attn_b_wq_w"]).astype(np.float32),
        "wkw": np.ascontiguousarray(inp["L0_attn_b_wk_w"]).astype(np.float32),
        "wvw": np.ascontiguousarray(inp["L0_attn_b_wv_w"]).astype(np.float32),
        "wow": np.ascontiguousarray(inp["L0_attn_b_wo_w"]).astype(np.float32),
        "wqb": inp["L0_attn_b_wq_b"].reshape(1, D).astype(np.float32),
        "wkb": inp["L0_attn_b_wk_b"].reshape(1, D).astype(np.float32),
        "wvb": inp["L0_attn_b_wv_b"].reshape(1, D).astype(np.float32),
        "wob": inp["L0_attn_b_wo_b"].reshape(D, 1).astype(np.float32),
        "wqiw": np.ascontiguousarray(inp["L0_attn_b_wqi_w"]).astype(np.float32),
        "wkiw": np.ascontiguousarray(inp["L0_attn_b_wki_w"]).astype(np.float32),
        "wqib": inp["L0_attn_b_wqi_b"].reshape(64, 1).astype(np.float32),
        "wkib": inp["L0_attn_b_wki_b"].reshape(64, 1).astype(np.float32),
        "identf": ident,
    }
    nc = _prog("b0", build_attn_b0)
    im = [dict(xf=full_xf(c), xq=sh[c], **wmap_b0) for c in cores]
    res = run_bass_kernel_spmd(nc, im, core_ids=cores)
    sh = [np.ascontiguousarray(res.results[c]["y"]) for c in cores]

    nc = _prog("moe", build_moe)
    im = [dict(x=sh[c], **wmap_f) for c in cores]
    res = run_bass_kernel_spmd(nc, im, core_ids=cores)
    sh = [np.ascontiguousarray(res.results[c]["y"]) for c in cores]

    wmap_a1 = {
        "wq": np.ascontiguousarray(inp["L1_attn_a_Wq"]).astype(np.float32),
        "wk": np.ascontiguousarray(inp["L1_attn_a_Wk"]).astype(np.float32),
        "wv": np.ascontiguousarray(inp["L1_attn_a_Wv"]).astype(np.float32),
    }
    nc = _prog("a1", build_attn_a1)
    im = []
    for c in cores:
        half = c % 2
        cm_c = np.ascontiguousarray(
            cm_full[:, half * Q:(half + 1) * Q]).astype(ml_dtypes.bfloat16)
        im.append(dict(xf=full_xf(c), xq=sh[c], cm=cm_c, **wmap_a1))
    res = run_bass_kernel_spmd(nc, im, core_ids=cores)
    sh = [np.ascontiguousarray(res.results[c]["y"]) for c in cores]

    wmap_g = {
        "wq": np.ascontiguousarray(inp["L1_ffn_Wq"]).astype(np.float32),
        "wk": np.ascontiguousarray(inp["L1_ffn_Wk"]).astype(np.float32),
        "wv": np.ascontiguousarray(inp["L1_ffn_Wv"]).astype(np.float32),
        "wqa": np.ascontiguousarray(inp["L1_ffn_Wqa"]).astype(np.float32),
        "wka": np.ascontiguousarray(inp["L1_ffn_Wka"]).astype(np.float32),
        "wga": np.ascontiguousarray(inp["L1_ffn_Wga"]).astype(np.float32),
        "wqb": np.ascontiguousarray(inp["L1_ffn_Wqb"]).astype(np.float32),
        "wkb": np.ascontiguousarray(inp["L1_ffn_Wkb"]).astype(np.float32),
        "wgb": np.ascontiguousarray(inp["L1_ffn_Wgb"]).astype(np.float32),
        "wg1a": np.ascontiguousarray(inp["L1_ffn_Wg1a"]).astype(np.float32),
        "wg2a": np.ascontiguousarray(inp["L1_ffn_Wg2a"]).astype(np.float32),
        "wg1b": np.ascontiguousarray(inp["L1_ffn_Wg1b"]).astype(np.float32),
        "wg2b": np.ascontiguousarray(inp["L1_ffn_Wg2b"]).astype(np.float32),
        "bg1r": inp["L1_ffn_bg1"].reshape(512, 1).astype(np.float32),
        "bg2r": inp["L1_ffn_bg2"].reshape(512, 1).astype(np.float32),
        "we": np.ascontiguousarray(inp["L1_ffn_We"]).astype(np.float32),
        "normwb": np.tile(inp["L1_ffn_norm_w"].astype(np.float32), 8).reshape(1, 512),
        "wo": np.ascontiguousarray(inp["L1_ffn_Wo"]).astype(np.float32),
        "trilm": trilc, "identf": ident, "blk2Tc": blk2Tc,
    }
    nc = _prog("gla", build_gla)
    im = [dict(xf=full_xf(c), **wmap_g) for c in cores]
    res = run_bass_kernel_spmd(nc, im, core_ids=cores)
    sh = [np.ascontiguousarray(
        res.results[c]["y"][:, (c % 2) * Q:(c % 2 + 1) * Q]) for c in cores]

    wmap_b1 = {
        "wkT": np.ascontiguousarray(inp["L1_attn_b_Wk"].T).astype(np.float32),
        "wqT": np.ascontiguousarray(inp["L1_attn_b_Wq"].T).astype(np.float32),
        "wvT": np.ascontiguousarray(inp["L1_attn_b_Wv"].T).astype(np.float32),
        "woT": np.ascontiguousarray(inp["L1_attn_b_Wo"].T).astype(np.float32),
        "bo": inp["L1_attn_b_bo"].reshape(D, 1).astype(np.float32),
        "klng2": np.tile(inp["L1_attn_b_kln_g"].astype(np.float32), 2).reshape(128, 1),
        "klnb2": np.tile(inp["L1_attn_b_kln_b"].astype(np.float32), 2).reshape(128, 1),
        "qlng2": np.tile(inp["L1_attn_b_qln_g"].astype(np.float32), 2).reshape(128, 1),
        "qlnb2": np.tile(inp["L1_attn_b_qln_b"].astype(np.float32), 2).reshape(128, 1),
        "blk2Tc": blk2Tc,
    }
    nc = _prog("b1", build_attn_b1)
    im = []
    for c in cores:
        half = c % 2
        cm_c = np.ascontiguousarray(
            cm_full[:, half * Q:(half + 1) * Q]).astype(ml_dtypes.bfloat16)
        im.append(dict(xf=full_xf(c), xq=sh[c], cm=cm_c, **wmap_b1))
    res = run_bass_kernel_spmd(nc, im, core_ids=cores)
    sh = [np.ascontiguousarray(res.results[c]["y"]) for c in cores]

    nc = _prog("gla", build_gla)
    im = [dict(xf=full_xf(c), **wmap_g) for c in cores]
    res = run_bass_kernel_spmd(nc, im, core_ids=cores)
    sh = [np.ascontiguousarray(
        res.results[c]["y"][:, (c % 2) * Q:(c % 2 + 1) * Q]) for c in cores]

    out = np.empty((B, Sn, Dm), np.float32)
    for c in cores:
        out[c // 2, (c % 2) * Q:(c % 2 + 1) * Q] = sh[c].T
    return out
